# version 04 (final)
# baseline (speedup 1.0000x reference)
import numpy as np
import ml_dtypes as mld
import concourse.bass as bass
import concourse.mybir as mybir
from concourse.bass_utils import run_bass_kernel_spmd
from contextlib import ExitStack

F32 = mybir.dt.float32
BF16 = mybir.dt.bfloat16
U32 = mybir.dt.uint32
AF = mybir.ActivationFunctionType
OP = mybir.AluOpType

B, T, D = 8, 2048, 1024
NCORES = 8
NCH = D // 128
NTT = T // 128
NT4 = T // 512
LN_EPS = 1e-5
MOE_CAP = 5120.0
EXP_SHIFT = -14.0


class Prog:
    QCH = {"q": 6, "g": 4, "s": 2}

    def __init__(self, nc):
        self.nc = nc
        self.ops = {e: [] for e in "tvsgq"}
        self.cnt = {k: 0 for k in "tvsg"}
        self.sems = {}
        self.lastw = {}
        self.dmacnt = {}
        self.rr = {"q": 0, "g": 0, "s": 0}
        for qn, n in self.QCH.items():
            for i in range(n):
                self.dmacnt[f"{qn}d{i}"] = 0
        self.qkind = None

    def make_sems(self, ctx):
        for k in list(self.cnt) + list(self.dmacnt):
            self.sems[k] = ctx.enter_context(self.nc.semaphore(f"sem_{k}"))

    def _app(self, eng, fn, dma=False, kind="op"):
        rec = {"fn": fn, "inc": None, "dma": dma, "kind": kind, "tok": None}
        self.ops[eng].append(rec)
        return rec

    def op(self, eng, fn):
        return self._app(eng, fn)

    def _dma_common(self, eng, fn, ch):
        if ch is None:
            ch = self.rr[eng]
            self.rr[eng] = (ch + 1) % self.QCH[eng]
        sem = f"{eng}d{ch}"
        if self.dmacnt[sem] > 0:
            self.wait(eng, (sem, 16 * self.dmacnt[sem]))
        rec = self._app(eng, fn, dma=True, kind="dma")
        self.dmacnt[sem] += 1
        rec["inc"] = (sem, 16)
        rec["tok"] = (sem, 16 * self.dmacnt[sem])
        return rec["tok"]

    def dma(self, eng, out, in_, slow=False, ch=None):
        if eng == "q":
            self._q_mode("plain", sbuf2sbuf=True)
        if slow:
            return self._dma_common(eng, lambda e: e.dma_start(
                out=out, in_=in_, allow_slow_non_contiguous=True), ch)
        return self._dma_common(eng, lambda e: e.dma_start(out=out, in_=in_), ch)

    def dma_tr(self, eng, out, in_, ch=None):
        assert eng == "q"
        self._q_mode("xbar", sbuf2sbuf=True)
        return self._dma_common(eng, lambda e: e.dma_start_transpose(out=out, in_=in_), ch)

    @staticmethod
    def _is_sbuf(ap):
        try:
            return "SB" in str(ap.tensor.memory_type)
        except Exception:
            return True

    def dma_quiesce(self, eng):
        for t in self.dma_tokens(eng):
            self.wait(eng, t)

    def dma_tokens(self, eng):
        return [(f"{eng}d{i}", 16 * self.dmacnt[f"{eng}d{i}"])
                for i in range(self.QCH[eng]) if self.dmacnt[f"{eng}d{i}"] > 0]

    def _q_mode(self, kind, sbuf2sbuf):
        if not sbuf2sbuf:
            return
        if self.qkind is not None and self.qkind != kind:
            self.dma_quiesce("q")
        self.qkind = kind

    def ms(self, eng):
        for rec in reversed(self.ops[eng]):
            if rec["kind"] == "wait":
                continue
            if rec["kind"] == "dma":
                return rec["tok"]
            if rec["inc"] is None:
                self.cnt[eng] += 1
                rec["inc"] = (eng, 1)
                rec["tok"] = (eng, self.cnt[eng])
            return rec["tok"]
        return None

    def wait(self, eng, tok):
        if tok is None:
            return
        sem, val = tok
        if sem == eng:
            return
        key = (eng, sem)
        if self.lastw.get(key, 0) >= val:
            return
        self.lastw[key] = val
        self._app(eng, lambda e: e.wait_ge(self.sems[sem], val), kind="wait")

    def waits(self, eng, toks):
        for t in toks:
            self.wait(eng, t)

    def selfsync(self, eng):
        tok = self.ms(eng)
        if tok is None:
            return
        sem, val = tok
        self._app(eng, lambda e: e.wait_ge(self.sems[sem], val), kind="wait")

    def barrier(self, engs="tvsgq"):
        toks = []
        for e in engs:
            if e in "tvsg" and self.ops[e]:
                tk = self.ms(e)
                if tk is not None:
                    toks.append(tk)
            if e in self.QCH:
                toks.extend(self.dma_tokens(e))
        for e in engs:
            for t in toks:
                self.wait(e, t)

    def finalize(self, block):
        def run(eng_name, eng_obj):
            for rec in self.ops[eng_name]:
                inst = rec["fn"](eng_obj)
                if rec["inc"] is not None:
                    sem, n = rec["inc"]
                    inst.then_inc(self.sems[sem], n)

        block.tensor(lambda t: run("t", t))
        block.vector(lambda v: run("v", v))
        block.scalar(lambda s: run("s", s))
        block.gpsimd(lambda g: run("g", g))
        block.sync(lambda q: run("q", q))


def _consts_np():
    c = {}
    c["identf"] = np.eye(128, dtype=np.float32)
    c["onesf"] = np.ones((128, 128), dtype=np.float32)
    c["onesb"] = np.ones((128, 128), dtype=mld.bfloat16)
    c["lstrict"] = np.triu(np.ones((128, 128), dtype=np.float32), k=1).astype(mld.bfloat16)
    u = np.arange(896)
    s = np.arange(128)[:, None]
    c["mext"] = (s <= (u[None, :] - 384)).astype(mld.bfloat16)
    c["ebc"] = np.broadcast_to(np.arange(8, dtype=np.float32), (128, 8)).copy()
    esel = np.zeros((8, 1024), dtype=np.float32)
    for e in range(8):
        esel[e, e * 128:(e + 1) * 128] = 1.0
    c["esel"] = esel.astype(mld.bfloat16)
    c["cpool"] = np.array([LN_EPS, EXP_SHIFT, 1.0 / 1024.0, 0.125, MOE_CAP,
                           1.0, -1.0, 0.0, 0.08838834764831845, -8.0
                           ], dtype=np.float32)[None, :].repeat(128, 0).copy()
    c["czb"] = np.array([1.0, 0.0], dtype=mld.bfloat16)[None, :].repeat(128, 0).copy()
    t64 = np.arange(T, dtype=np.float64)
    inv_f = 1.0 / 10000.0 ** (np.arange(0, 64, 2, dtype=np.float64) / 64)
    emb = np.concatenate([np.outer(t64, inv_f), np.outer(t64, inv_f)], -1)
    cs = np.cos(emb).T; sn = np.sin(emb).T
    c["rcos"] = np.concatenate([cs, cs], 0).astype(mld.bfloat16)
    c["rsin"] = np.concatenate([sn, sn], 0).astype(mld.bfloat16)
    gam = 1.0 - np.exp2(-5.0 - np.arange(8, dtype=np.float64))
    gq = np.zeros((128, 4 * T)); gk = np.zeros((128, 4 * T))
    for ch in range(4):
        for half, h in ((0, 2 * ch), (64, 2 * ch + 1)):
            gq[half:half + 64, ch * T:(ch + 1) * T] = gam[h] ** t64[None, :]
            gk[half:half + 64, ch * T:(ch + 1) * T] = gam[h] ** (-t64[None, :])
    c["gqt"] = gq.astype(mld.bfloat16)
    c["gkt"] = gk.astype(mld.bfloat16)
    hi = np.arange(0, 64, 2, dtype=np.float64)
    invf2 = 1.0 / 10000.0 ** (hi / 64)
    fr = np.concatenate([np.outer(t64, invf2), np.outer(t64, invf2)], -1)
    base = (hi + 0.4 * 64) / (1.4 * 64)
    power = (t64 - T // 2) / 256.0
    scl = base[None, :] ** power[:, None]
    scl = np.concatenate([scl, scl], -1)
    xqc = (np.cos(fr) * scl).T; xqs = (np.sin(fr) * scl).T
    xkc = (np.cos(fr) / scl).T; xks = (np.sin(fr) / scl).T
    c["xqc"] = np.concatenate([xqc, xqc], 0).astype(mld.bfloat16)
    c["xqs"] = np.concatenate([xqs, xqs], 0).astype(mld.bfloat16)
    c["xkc"] = np.concatenate([xkc, xkc], 0).astype(mld.bfloat16)
    c["xks"] = np.concatenate([xks, xks], 0).astype(mld.bfloat16)
    ind = np.zeros((128, 2)); ind[0:64, 0] = 1; ind[64:128, 1] = 1
    c["ind2c"] = ind.astype(mld.bfloat16)
    c["ind2r"] = ind.T.copy().astype(mld.bfloat16)
    s_ = np.arange(128)[:, None]; u_ = np.arange(128)[None, :]
    c["mrev"] = (s_ >= u_).astype(mld.bfloat16)
    return c


CP = {"eps": 0, "shift": 1, "inv1024": 2, "qscale": 3, "cap": 4, "one": 5, "neg1": 6, "zero": 7, "qk128": 8, "shift8": 9}


def build_program(nlayers=1, taps=(), phase='full'):
    nc = bass.Bass("TRN2", target_bir_lowering=False, detect_race_conditions=False)
    xo_d = nc.dram_tensor("xown", [T, D], F32, kind="ExternalInput").ap()
    xf_d = nc.dram_tensor("xfull", [B, T, D], F32, kind="ExternalInput").ap()
    om_d = nc.dram_tensor("ownmask", [1, B], F32, kind="ExternalInput").ap()
    y_d = nc.dram_tensor("y", [T, D], F32, kind="ExternalOutput").ap()
    rsc_d = nc.dram_tensor("rowscratch", [2, T], F32).ap()

    wd = {}
    for name, shape in [
        ("L0_attn_gamma", [D]), ("L0_attn_beta", [D]), ("L0_attn_w_qkv", [3 * D, D]),
        ("L0_ffn_router_w", [D, 8]), ("L0_ffn_router_b", [1, 8]),
        ("L0_ffn_w1", [8, D, 2048]), ("L0_ffn_b1", [8, 2048]),
        ("L0_ffn_w2", [8, 2048, D]), ("L0_ffn_b2", [8, D]),
    ]:
        wd[name] = nc.dram_tensor(name, shape, F32, kind="ExternalInput").ap()
    if nlayers >= 2:
        for name, shape in [
            ("L1_attn_Wq", [D, D]), ("L1_attn_bq", [D]),
            ("L1_attn_Wk", [D, D]), ("L1_attn_bk", [D]),
            ("L1_attn_Wv", [D, D]), ("L1_attn_bv", [1, D]),
            ("L1_attn_Wproj", [D, D]), ("L1_attn_bproj", [D]),
            ("L1_attn_wq_feat", [64, 256]), ("L1_attn_wk_feat", [64, 256]),
            ("L1_ffn_router_w", [D, 8]), ("L1_ffn_router_b", [1, 8]),
            ("L1_ffn_w1", [8, D, 256]), ("L1_ffn_w2", [8, 256, 4096]),
            ("L1_ffn_lin2_w", [4096, D]), ("L1_ffn_lin2_b", [D]),
        ]:
            wd[name] = nc.dram_tensor(name, shape, F32, kind="ExternalInput").ap()
    if nlayers >= 3:
        for name, shape in [
            ("L2_attn_proj_qk_w", [D, D]), ("L2_attn_proj_qk_b", [D]),
            ("L2_attn_proj_v_w", [D, D]), ("L2_attn_proj_v_b", [D]),
            ("L2_attn_in_proj_w", [3 * D, D]), ("L2_attn_in_proj_b", [3 * D]),
            ("L2_attn_out_proj_w", [D, D]), ("L2_attn_out_proj_b", [D]),
            ("L2_ffn_W_r", [D, D]), ("L2_ffn_W_k", [D, D]), ("L2_ffn_W_v", [D, D]),
            ("L2_ffn_mu_r", [D]), ("L2_ffn_mu_k", [D]),
        ]:
            wd[name] = nc.dram_tensor(name, shape, F32, kind="ExternalInput").ap()
    if nlayers >= 4:
        for name, shape in [
            ("L3_attn_Wq", [512, D]), ("L3_attn_Wk", [512, D]),
            ("L3_attn_Wv", [D, D]), ("L3_attn_Wg", [D, D]),
            ("L3_attn_Wo", [D, D]),
            ("L3_attn_gn_weight", [D]), ("L3_attn_gn_bias", [D]),
            ("L3_ffn_wqkv", [2, D, 1536]), ("L3_ffn_q_scale", [2, 64]),
            ("L3_ffn_k_scale", [2, 64]), ("L3_ffn_wo", [2, 512, D]),
            ("L3_ffn_ln_g", [2, D]), ("L3_ffn_ln_b", [2, D]),
            ("L3_ffn_w1", [2, D, 5460]), ("L3_ffn_b1", [2, 5460]),
            ("L3_ffn_w2", [2, 2730, D]), ("L3_ffn_b2", [2, D]),
        ]:
            wd[name] = nc.dram_tensor(name, shape, F32, kind="ExternalInput").ap()

    cnp = _consts_np()
    cd = {k: nc.inline_tensor(v, name=k + "_c").ap() for k, v in cnp.items()}

    taps = set(taps)
    tap_outs = {}

    p = Prog(nc)
    ctx = ExitStack()
    if True:
        def sb(name, shape, dt=F32):
            return ctx.enter_context(nc.sbuf_tensor(name, shape, dt))

        def ps(name, shape, dt=F32):
            return ctx.enter_context(nc.psum_tensor(name, shape, dt))

        p.make_sems(ctx)

        x = sb("x", [128, NCH * T])
        xn = sb("xn", [128, NCH * T], BF16)
        identf = sb("identf", [128, 128])
        onesf = sb("onesf", [128, 128])
        onesb = sb("onesb", [128, 128], BF16)
        lstrict = sb("lstrict", [128, 128], BF16)
        mext = sb("mext", [128, 896], BF16)
        ebc = sb("ebc", [128, 8])
        esel = sb("esel", [8, 1024], BF16)
        cpool = sb("cpool", [128, 10])
        czb = sb("czb", [128, 2], BF16)
        omask = sb("omask", [1, B])
        nc.const_aps.register(F32, 1.0, cpool[:, 5:6])
        nc.const_aps.register(F32, -1.0, cpool[:, 6:7])
        nc.const_aps.register(F32, 0.0, cpool[:, 7:8])
        nc.const_aps.register(BF16, 1.0, czb[:, 0:1])
        nc.const_aps.register(BF16, 0.0, czb[:, 1:2])
        rA = sb("rowA", [1, T])
        rB = sb("rowB", [1, T])
        rC = sb("rowC", [1, T])
        rS = sb("rowS", [1, 3 * 512])
        rL = sb("rowL", [1, 512])
        tscr = sb("tscr", [128, 512])
        pb = [ps(f"pb{i}", [128, 512]) for i in range(8)]

        def rs(i, n=512):
            return rS[0:1, i * 512: i * 512 + n]

        def cp(name):
            return cpool[:, CP[name]:CP[name] + 1]

        def tap(name, ap, shape, dt=F32):
            if name in taps:
                tap_outs[name] = nc.dram_tensor("tap_" + name, shape, dt,
                                                kind="ExternalOutput").ap()
                p.wait("q", p.ms(ap_eng[0]))
                return p.dma("q", tap_outs[name], ap)

        ap_eng = ["v"]

        block = ctx.enter_context(nc.Block())

        ctoks = []
        for k in cd:
            m_ = {"identf": identf, "onesf": onesf, "onesb": onesb,
                  "lstrict": lstrict, "mext": mext, "ebc": ebc, "esel": esel,
                  "cpool": cpool, "czb": czb}
            if k not in m_:
                continue
            dst = m_[k]
            ctoks.append(p.dma("q", dst[:], cd[k]))
        ctoks.append(p.dma("q", omask[:], om_d))
        ctok = ctoks[-1]
        for e in "tvsg":
            p.waits(e, ctoks)

        ldstack = ExitStack()
        tmb = [ldstack.enter_context(nc.sbuf_tensor(f"tmb{i}", [128, D], F32))
               for i in range(2)]
        ev_toks = {}
        for i in range(NTT):
            buf = tmb[i % 2]
            if i >= 2:
                p.wait("q", ev_toks[i - 2])
            ld = p.dma("q", buf[:], xo_d[i * 128:(i + 1) * 128, :])
            p.wait("t", ld)
            p.wait("t", ctoks[0])
            for c in range(NCH):
                bank = pb[c % 2]
                p.op("t", lambda t, b=buf, c=c, bank=bank: t.transpose(
                    bank[:, 0:128], in_=b[:, c * 128:(c + 1) * 128], identity=identf[:]))
                ttok = p.ms("t")
                eng = "v" if c % 2 == 0 else "s"
                p.wait(eng, ttok)
                if eng == "v":
                    p.op("v", lambda v, i=i, c=c, bank=bank: v.tensor_copy(
                        out=x[:, c * T + i * 128: c * T + (i + 1) * 128], in_=bank[:, 0:128]))
                else:
                    p.op("s", lambda s, i=i, c=c, bank=bank: s.copy(
                        out=x[:, c * T + i * 128: c * T + (i + 1) * 128], in_=bank[:, 0:128]))
                p.wait("t", p.ms(eng))
            ev_toks[i] = p.ms("v" if (NCH - 1) % 2 == 0 else "s")
        p.barrier()
        ldstack.close()

        def ln_stats(src_getter, a_row, b_row, sq_scratch, xbf_getter=None,
                     rstd2_row=None, mu_row=None, rstd_row=None):
            for tt in range(NT4):
                sl = slice(tt * 512, (tt + 1) * 512)
                if tt > 0:
                    p.wait("t", ("v", p.cnt["v"]))
                    p.wait("t", ("s", p.cnt["s"]))
                mm_toks = {}
                for c in range(NCH):
                    srcb = xbf_getter(c, tt) if xbf_getter else None
                    if srcb is None:
                        buf = sq_scratch[(c % 2) + 2]
                        if c >= 2:
                            p.wait("v", mm_toks[c - 2])
                        p.op("v", lambda v, c=c, sl=sl, buf=buf: v.tensor_copy(
                            out=buf[:, 0:512], in_=src_getter(c)[:, sl]))
                        p.wait("t", p.ms("v"))
                        rhs = buf[:, 0:512]
                    else:
                        rhs = srcb
                    p.op("t", lambda t, c=c, rhs=rhs: t.matmul(
                        pb[6][0:1, :], lhsT=onesb[:, 0:1], rhs=rhs,
                        start=(c == 0), stop=(c == NCH - 1)))
                    mm_toks[c] = p.ms("t")
                sq_toks = {}
                for c in range(NCH):
                    buf = sq_scratch[c % 2]
                    if c >= 2:
                        p.wait("s", sq_toks[c - 2])
                    p.op("s", lambda s, c=c, sl=sl, buf=buf: s.activation(
                        out=buf[:, 0:512], in_=src_getter(c)[:, sl], func=AF.Square))
                    p.wait("t", p.ms("s"))
                    p.op("t", lambda t, c=c, buf=buf: t.matmul(
                        pb[7][0:1, :], lhsT=onesb[:, 0:1], rhs=buf[:, 0:512],
                        start=(c == 0), stop=(c == NCH - 1)))
                    sq_toks[c] = p.ms("t")
                ttok = p.ms("t")
                p.wait("v", ttok)
                p.op("v", lambda v: v.tensor_scalar_mul(rs(0), pb[6][0:1, :], 1.0 / 1024.0))
                p.op("v", lambda v: v.tensor_scalar_mul(rs(1), pb[7][0:1, :], 1.0 / 1024.0))
                p.op("v", lambda v: v.tensor_mul(rs(2), rs(0), rs(0)))
                p.op("v", lambda v: v.tensor_sub(rs(1), rs(1), rs(2)))
                p.op("v", lambda v: v.tensor_scalar_add(rs(2), rs(1), LN_EPS))
                p.wait("s", p.ms("v"))
                p.op("s", lambda s: s.activation(out=rs(2), in_=rs(2), func=AF.Sqrt))
                p.wait("v", p.ms("s"))
                arow_sl = a_row[0:1, tt * 512:(tt + 1) * 512]
                brow_sl = b_row[0:1, tt * 512:(tt + 1) * 512]
                p.op("v", lambda v, arow_sl=arow_sl: v.reciprocal(arow_sl, rs(2)))
                p.selfsync("v")
                p.op("v", lambda v, arow_sl=arow_sl, brow_sl=brow_sl: v.tensor_mul(
                    brow_sl, rs(0), arow_sl))
                p.op("v", lambda v, brow_sl=brow_sl: v.tensor_scalar_mul(
                    brow_sl, brow_sl, -1.0))
                if mu_row is not None:
                    p.op("v", lambda v, tt=tt: v.tensor_copy(
                        out=mu_row[0:1, tt * 512:(tt + 1) * 512], in_=rs(0)))
                if rstd_row is not None:
                    p.op("v", lambda v, tt=tt, arow_sl=arow_sl: v.tensor_copy(
                        out=rstd_row[0:1, tt * 512:(tt + 1) * 512], in_=arow_sl))
                if rstd2_row is not None:
                    p.op("v", lambda v: v.tensor_scalar_add(rs(2), rs(1), LN_EPS))
                    p.op("v", lambda v: v.reciprocal(rs(2), rs(2)))
                    p.selfsync("v")
                    p.op("v", lambda v: v.tensor_mul(rs(2), rs(1), rs(2)))
                    p.op("v", lambda v: v.tensor_scalar_add(rs(2), rs(2), LN_EPS))
                    p.wait("s", p.ms("v"))
                    p.op("s", lambda s: s.activation(out=rs(2), in_=rs(2), func=AF.Sqrt))
                    p.wait("v", p.ms("s"))
                    p.op("v", lambda v, tt=tt: v.reciprocal(
                        rstd2_row[0:1, tt * 512:(tt + 1) * 512], rs(2)))
                p.wait("t", p.ms("v"))

        def bcast_row(row_ap, npart, width, bank, accum=False):
            p.op("t", lambda t: t.matmul(bank[0:npart, 0:width],
                                         lhsT=onesf[0:1, 0:npart],
                                         rhs=row_ap, start=not accum, stop=True))
            return p.ms("t")

        def ln_apply(src_getter, a_row, b_row, dst_getter, gscale=None, gbias=None, level=3):
            for tt in range(NT4):
                t1 = bcast_row(a_row[0:1, tt * 512:(tt + 1) * 512], 128, 512, pb[6])
                t2 = bcast_row(b_row[0:1, tt * 512:(tt + 1) * 512], 128, 512, pb[7])
                p.waits("v", [t1, t2])
                if level == 1:
                    continue
                for c in range(NCH):
                    sl = slice(tt * 512, (tt + 1) * 512)
                    dst = dst_getter(c)
                    p.op("v", lambda v, c=c, sl=sl: v.tensor_mul(
                        tscr[:, :], src_getter(c)[:, sl], pb[6][:, 0:512]))
                    if level == 2:
                        continue
                    if gscale is None:
                        p.op("v", lambda v, c=c, sl=sl, dst=dst: v.tensor_add(
                            dst[:, sl], tscr[:, :], pb[7][:, 0:512]))
                    else:
                        p.op("v", lambda v: v.tensor_add(
                            tscr[:, :], tscr[:, :], pb[7][:, 0:512]))
                        p.op("v", lambda v, c=c, sl=sl, dst=dst: v.tensor_scalar(
                            out=dst[:, sl], in0=tscr[:, :],
                            scalar1=gscale[:, c:c + 1], scalar2=gbias[:, c:c + 1],
                            op0=OP.mult, op1=OP.add))
                p.wait("t", p.ms("v"))

        class StopPhase(Exception):
            pass
        try:
          if phase == "load":
              raise StopPhase
          l0 = ExitStack()
          if True:
            sb0 = lambda n, s, d=F32: l0.enter_context(nc.sbuf_tensor(n, s, d))

            sqs = [sb0(f"sqs{i}", [128, 512], BF16) for i in range(4)]
            ln_stats(lambda c: x[:, c * T:(c + 1) * T], rA, rB, sqs, rstd2_row=rC)
            if phase == "lns":
                raise StopPhase
            ln_apply(lambda c: x[:, c * T:(c + 1) * T], rA, rB,
                     lambda c: xn[:, c * T:(c + 1) * T],
                     level={"lnap1": 1, "lnap2": 2}.get(phase, 3))
            if phase in ("lnap1", "lnap2"):
                raise StopPhase
            if phase == "ln":
                raise StopPhase
            ap_eng[0] = "v"
            tap("xn", xn[:, 0:T], [128, T], BF16)


            moest = ExitStack()
            sbm = lambda n, s, d=F32: moest.enter_context(nc.sbuf_tensor(n, s, d))
            wr_bf = sbm("wr_bf", [8, T], BF16)
            rstack = ExitStack()
            sbr = lambda n, s, d=F32: rstack.enter_context(nc.sbuf_tensor(n, s, d))
            TH = T // 2
            xf_fm = sbr("xf_fm", [128, NCH * TH], BF16)
            xf_lo = sbr("xf_lo", [128, NCH * TH], BF16)
            xtb = [sbr(f"xtb{i}", [128, D]) for i in range(2)]
            xhib = sbr("xhib", [128, D], BF16)
            xlob = sbr("xlob", [128, D], BF16)
            sum_col = sbr("sum_col", [128, 16])
            sq_col = sbr("sq_col", [128, 16])
            rw_bf = sbr("rw_bf", [128, 64], BF16)
            rw_lo = sbr("rw_lo", [128, 64], BF16)
            rwf = sbr("rwf", [128, 64])
            sqscr = sbr("sqscr", [128, D])
            c_b = sbr("c_b", [128, 8])
            b_b = sbr("b_b", [128, 8])
            omb = sbr("omb", [128, 8])
            mu_col = sbr("mu_col", [128, 16])
            nrstd_col = sbr("nrstd_col", [128, 16])
            lgs = sbr("lgs", [128, 8])
            t8a = sbr("t8a", [128, 8]); t8b = sbr("t8b", [128, 8])
            m8 = sbr("m8", [128, 8])
            idx8 = sbr("idx8", [128, 8], U32)
            idxf = sbr("idxf", [128, 2])
            oh0 = sbr("oh0", [128, 8]); oh1 = sbr("oh1", [128, 8])
            ohsf = sbr("ohsf", [128, 8])
            ohs_all = sbr("ohs_all", [128, 8 * B * NTT], BF16)
            oh0_all = sbr("oh0_all", [128, 8 * B * NTT], BF16)
            gcol_all = sbr("gcol_all", [128, B * NTT])
            w0_all = sbr("w0_all", [128, B * NTT])
            w1_all = sbr("w1_all", [128, B * NTT])
            cnt_row = sbr("cnt_row", [1, 1024])
            cnt_cols = sbr("cnt_cols", [128, 8])
            cnt_cols_bf = sbr("cnt_cols_bf", [128, 8], BF16)
            thr_sb = sbr("thr_sb", [128, 8])
            baserow = sbr("baserow", [1, 1024])
            keepm = sbr("keepm", [128, 8])
            colsc = sbr("colsc", [128, 6])
            w_own = sbr("w_own", [128, 128])

            ld1 = p.dma("q", rwf[:].rearrange("p (ec e) -> p ec e", ec=NCH),
                        wd["L0_ffn_router_w"].rearrange("(ec p) e -> p ec e", p=128))
            ld2 = p.dma("q", b_b[:], wd["L0_ffn_router_b"].to_broadcast((128, 8)))
            ld3 = p.dma("q", omb[:], om_d.to_broadcast((128, 8)))
            p.waits("v", [ld1, ld2, ld3])
            p.op("v", lambda v: v.tensor_copy(out=rw_bf[:], in_=rwf[:]))
            p.selfsync("v")
            p.op("v", lambda v: v.tensor_sub(rw_lo[:], rwf[:], rw_bf[:]))
            p.wait("t", p.ms("v"))
            for ec in range(NCH):
                p.op("t", lambda t, ec=ec: t.matmul(
                    pb[6][0:1, 0:8], lhsT=onesb[:, 0:1], rhs=rw_bf[:, ec * 8:(ec + 1) * 8],
                    start=(ec == 0), stop=(ec == NCH - 1)))
            p.wait("v", p.ms("t"))
            p.op("v", lambda v: v.tensor_copy(out=cnt_row[0:1, 0:8], in_=pb[6][0:1, 0:8]))
            p.wait("t", p.ms("v"))
            ct1 = bcast_row(cnt_row[0:1, 0:8], 128, 8, pb[7])
            p.wait("v", ct1)
            p.op("v", lambda v: v.tensor_copy(out=c_b[:], in_=pb[7][0:128, 0:8]))
            p.wait("t", p.ms("v"))

            NCHUNK_ALL = B * NTT

            rst = {"prev_tr": [], "bufdep": {0: [], 1: []}}

            def route_batch_pass1(b, half):
                p.wait("q", ("t", p.cnt["t"]))
                bufdep = rst["bufdep"]
                for ti in range(half * 8, half * 8 + 8):
                    buf = xtb[ti % 2]
                    p.waits("q", bufdep[ti % 2])
                    lt = p.dma("q", buf[:], xf_d[b, ti * 128:(ti + 1) * 128, :])
                    p.waits("s", [lt])
                    p.op("s", lambda s, buf=buf, ti=ti: s.activation(
                        out=sqscr[:], in_=buf[:], func=AF.Square,
                        accum_out=sq_col[:, ti:ti + 1]))
                    p.op("s", lambda s, buf=buf, ti=ti: s.activation(
                        out=sqscr[:], in_=buf[:], func=AF.Copy,
                        accum_out=sum_col[:, ti:ti + 1]))
                    stok = p.ms("s")
                    p.waits("v", [lt])
                    p.waits("v", rst["prev_tr"])
                    p.op("v", lambda v, buf=buf: v.tensor_copy(out=xhib[:], in_=buf[:]))
                    p.selfsync("v")
                    p.op("v", lambda v, buf=buf: v.tensor_sub(xlob[:], buf[:], xhib[:]))
                    vtk = p.ms("v")
                    p.wait("q", vtk)
                    tl = ti * 128 - half * TH
                    tr1 = p.dma_tr("q", xf_fm[:].rearrange(
                        "p (c t) -> p c t", c=NCH)[:, :, tl:tl + 128],
                        xhib[:].rearrange("t (c e) -> t c e", c=NCH))
                    tr2 = p.dma_tr("q", xf_lo[:].rearrange(
                        "p (c t) -> p c t", c=NCH)[:, :, tl:tl + 128],
                        xlob[:].rearrange("t (c e) -> t c e", c=NCH))
                    rst["prev_tr"] = [tr1, tr2]
                    bufdep[ti % 2] = [stok, vtk]
                p.waits("t", p.dma_tokens("q"))
                p.waits("s", p.dma_tokens("q"))
                p.waits("v", p.dma_tokens("q"))
                if b == 0 and half == 0 and "xffm" in taps:
                    tap_outs["xffm"] = nc.dram_tensor(
                        "tap_xffm", [128, NCH * TH], BF16, kind="ExternalOutput").ap()
                    tap_outs["scol"] = nc.dram_tensor(
                        "tap_scol", [128, 32], F32, kind="ExternalOutput").ap()
                    p.wait("q", ("s", p.cnt["s"]))
                    _x1 = p.dma("q", tap_outs["xffm"], xf_fm[:])
                    _x2 = p.dma("q", tap_outs["scol"][:, 0:16], sum_col[:])
                    _x3 = p.dma("q", tap_outs["scol"][:, 16:32], sq_col[:])
                    p.waits("v", [_x1, _x2, _x3])
                    p.waits("s", [_x2, _x3])
                    p.waits("t", [_x1])
                h0 = half * 8
                p.wait("v", p.ms("s"))
                p.op("v", lambda v, h0=h0: v.tensor_scalar_mul(
                    mu_col[:, h0:h0 + 8], sum_col[:, h0:h0 + 8], 1.0 / 1024.0))
                p.op("v", lambda v, h0=h0: v.tensor_scalar_mul(
                    nrstd_col[:, h0:h0 + 8], sq_col[:, h0:h0 + 8], 1.0 / 1024.0))
                p.op("v", lambda v, h0=h0: v.tensor_mul(
                    t8a[:], mu_col[:, h0:h0 + 8], mu_col[:, h0:h0 + 8]))
                p.op("v", lambda v, h0=h0: v.tensor_sub(
                    nrstd_col[:, h0:h0 + 8], nrstd_col[:, h0:h0 + 8], t8a[:]))
                p.op("v", lambda v, h0=h0: v.tensor_scalar_add(
                    nrstd_col[:, h0:h0 + 8], nrstd_col[:, h0:h0 + 8], LN_EPS))
                p.wait("s", p.ms("v"))
                p.op("s", lambda s, h0=h0: s.activation(
                    out=nrstd_col[:, h0:h0 + 8], in_=nrstd_col[:, h0:h0 + 8], func=AF.Sqrt))
                p.wait("v", p.ms("s"))
                p.op("v", lambda v, h0=h0: v.reciprocal(
                    nrstd_col[:, h0:h0 + 8], nrstd_col[:, h0:h0 + 8]))
                p.selfsync("v")
                p.wait("t", p.ms("v"))
                for i in range(half * 8, half * 8 + 8):
                    gi = b * NTT + i
                    il = i - half * 8
                    p.wait("t", ("v", p.cnt["v"]))
                    for ec in range(NCH):
                        p.op("t", lambda t, ec=ec, il=il: t.matmul(
                            pb[2][0:128, 0:8],
                            lhsT=xf_fm[:, ec * TH + il * 128: ec * TH + (il + 1) * 128],
                            rhs=rw_bf[:, ec * 8:(ec + 1) * 8],
                            start=(ec == 0), stop=False))
                    for ec in range(NCH):
                        p.op("t", lambda t, ec=ec, il=il: t.matmul(
                            pb[2][0:128, 0:8],
                            lhsT=xf_lo[:, ec * TH + il * 128: ec * TH + (il + 1) * 128],
                            rhs=rw_bf[:, ec * 8:(ec + 1) * 8],
                            start=False, stop=False))
                    for ec in range(NCH):
                        p.op("t", lambda t, ec=ec, il=il: t.matmul(
                            pb[2][0:128, 0:8],
                            lhsT=xf_fm[:, ec * TH + il * 128: ec * TH + (il + 1) * 128],
                            rhs=rw_lo[:, ec * 8:(ec + 1) * 8],
                            start=False, stop=(ec == NCH - 1)))
                    rawt = p.ms("t")
                    p.wait("v", rawt)
                    p.op("v", lambda v, i=i: v.tensor_scalar(
                        out=t8a[:], in0=c_b[:], scalar1=mu_col[:, i:i + 1],
                        scalar2=None, op0=OP.mult))
                    p.selfsync("v")
                    p.op("v", lambda v: v.tensor_sub(lgs[:], pb[2][0:128, 0:8], t8a[:]))
                    p.selfsync("v")
                    p.op("v", lambda v, i=i: v.tensor_scalar(
                        out=lgs[:], in0=lgs[:], scalar1=nrstd_col[:, i:i + 1],
                        scalar2=None, op0=OP.mult))
                    p.selfsync("v")
                    p.op("v", lambda v: v.tensor_add(lgs[:], lgs[:], b_b[:]))
                    p.selfsync("v")
                    if b == 0 and i == 0 and "lgs0" in taps:
                        tap_outs["lgs0"] = nc.dram_tensor(
                            "tap_lgs0", [128, 8], F32, kind="ExternalOutput").ap()
                        tap_outs["raw0"] = nc.dram_tensor(
                            "tap_raw0", [128, 8], F32, kind="ExternalOutput").ap()
                        tap_outs["mu0"] = nc.dram_tensor(
                            "tap_mu0", [128, 8], F32, kind="ExternalOutput").ap()
                        tap_outs["nr0"] = nc.dram_tensor(
                            "tap_nr0", [128, 8], F32, kind="ExternalOutput").ap()
                        p.op("v", lambda v: v.tensor_copy(out=t8b[:], in_=pb[2][0:128, 0:8]))
                        tvv = p.ms("v")
                        p.wait("q", tvv)
                        _t1 = p.dma("q", tap_outs["lgs0"], lgs[:])
                        _t2 = p.dma("q", tap_outs["raw0"], t8b[:])
                        _t3 = p.dma("q", tap_outs["mu0"], mu_col[:, 0:8])
                        _t4 = p.dma("q", tap_outs["nr0"], nrstd_col[:, 0:8])
                        p.waits("v", [_t1, _t2, _t3, _t4])
                        p.waits("t", [_t2])
                    p.op("v", lambda v: v.max(m8[:], lgs[:]))
                    p.selfsync("v")
                    p.op("v", lambda v: v.tensor_scalar(
                        out=oh0[:], in0=lgs[:], scalar1=m8[:, 0:1], scalar2=None,
                        op0=OP.is_equal))
                    p.selfsync("v")
                    p.op("v", lambda v: v.tensor_scalar(
                        out=oh1[:], in0=lgs[:], scalar1=m8[:, 1:2], scalar2=None,
                        op0=OP.is_equal))
                    p.selfsync("v")
                    p.op("v", lambda v, gi=gi: v.tensor_add(
                        ohsf[:], oh0[:], oh1[:]))
                    p.selfsync("v")
                    p.op("v", lambda v, gi=gi: v.tensor_copy(
                        out=ohs_all[:, gi * 8:(gi + 1) * 8], in_=ohsf[:]))
                    p.op("v", lambda v, gi=gi: v.tensor_copy(
                        out=oh0_all[:, gi * 8:(gi + 1) * 8], in_=oh0[:]))
                    p.op("v", lambda v, gi=gi: v.tensor_sub(
                        gcol_all[:, gi:gi + 1], m8[:, 0:1], m8[:, 1:2]))
                    p.ms("v")
                    if b == 0 and i == 5 and "chain5" in taps:
                        tap_outs["chain5"] = nc.dram_tensor(
                            "tap_chain5", [128, 40], F32, kind="ExternalOutput").ap()
                        p.selfsync("v")
                        p.wait("q", p.ms("v"))
                        _c1 = p.dma("q", tap_outs["chain5"][:, 0:8], lgs[:])
                        _c2 = p.dma("q", tap_outs["chain5"][:, 8:16], m8[:])
                        _c3 = p.dma("q", tap_outs["chain5"][:, 16:24], oh0[:])
                        _c4 = p.dma("q", tap_outs["chain5"][:, 24:32], ohsf[:])
                        p.wait("g", p.ms("v"))
                        _c5 = p.dma("g", tap_outs["chain5"][:, 32:40],
                                    ohs_all[:, gi * 8:(gi + 1) * 8])
                        p.waits("v", [_c1, _c2, _c3, _c4, _c5])

            for b in range(B):
                for half in range(2):
                    route_batch_pass1(b, half)
            p.barrier()

            p.op("t", lambda t: t.matmul(pb[6][0:1, 0:512], lhsT=onesb[:, 0:1],
                                         rhs=ohs_all[:, 0:512], start=True, stop=True))
            p.op("t", lambda t: t.matmul(pb[7][0:1, 0:512], lhsT=onesb[:, 0:1],
                                         rhs=ohs_all[:, 512:1024], start=True, stop=True))
            ctk = p.ms("t")
            p.wait("v", ctk)
            p.op("v", lambda v: v.tensor_copy(out=cnt_row[0:1, 0:512], in_=pb[6][0:1, 0:512]))
            p.op("v", lambda v: v.tensor_copy(out=cnt_row[0:1, 512:1024], in_=pb[7][0:1, 0:512]))
            cvt = p.ms("v")
            p.wait("q", cvt)
            p.dma_quiesce("q")
            b1_ = p.dma("q", rsc_d[0:1, 0:1024], cnt_row[0:1, :])
            p.waits("q", [b1_])
            b2_ = p.dma("q", cnt_cols[:], rsc_d[0:1, 0:1024].rearrange(
                "o (pp e) -> (o pp) e", pp=128))
            p.waits("v", [b2_])
            p.op("v", lambda v: v.tensor_copy(out=cnt_cols_bf[:], in_=cnt_cols[:]))
            p.wait("t", p.ms("v"))
            p.op("t", lambda t: t.matmul(pb[3][0:128, 0:8], lhsT=lstrict[:],
                                         rhs=cnt_cols_bf[:], start=True, stop=True))
            btk = p.ms("t")
            p.wait("v", btk)
            p.op("v", lambda v: v.tensor_copy(out=cnt_cols[:], in_=pb[3][0:128, 0:8]))
            bvt = p.ms("v")
            p.wait("q", bvt)
            p.dma_quiesce("q")
            b3_ = p.dma("q", rsc_d[1:2, 0:1024], cnt_cols[:].rearrange("pp e -> (pp e)") if False else cnt_cols[:])
            p.waits("q", [b3_])
            b4_ = p.dma("q", baserow[0:1, :], rsc_d[1:2, 0:1024])
            p.waits("t", [b4_])
            p.waits("v", [b4_])
            p.op("v", lambda v: v.tensor_scalar(
                out=baserow[0:1, :], in0=baserow[0:1, :], scalar1=-1.0,
                scalar2=MOE_CAP, op0=OP.mult, op1=OP.add))
            p.wait("t", p.ms("v"))

            if "omb" in taps:
                tap_outs["omb"] = nc.dram_tensor("tap_omb", [128, 8], F32,
                                                 kind="ExternalOutput").ap()
                tap_outs["ohs0"] = nc.dram_tensor("tap_ohs0", [128, 128], BF16,
                                                  kind="ExternalOutput").ap()
                tap_outs["gcol0"] = nc.dram_tensor("tap_gcol0", [128, 16], F32,
                                                   kind="ExternalOutput").ap()
                p.dma("q", tap_outs["omb"], omb[:])
                p.dma("q", tap_outs["ohs0"], ohs_all[:, 0:128])
                _g = p.dma("q", tap_outs["gcol0"], gcol_all[:, 0:16])
                p.waits("v", [_g])
            if "baserow" in taps:
                tap_outs["baserow"] = nc.dram_tensor("tap_baserow", [1, 1024], F32,
                                                     kind="ExternalOutput").ap()
                p.waits("q", [b4_])
                p.dma("q", tap_outs["baserow"], baserow[0:1, :])
                tap_outs["cntrow"] = nc.dram_tensor("tap_cntrow", [1, 1024], F32,
                                                    kind="ExternalOutput").ap()
                p.dma("q", tap_outs["cntrow"], cnt_row[0:1, :])
            p.op("s", lambda s: s.activation(out=w0_all[:], in_=gcol_all[:], func=AF.Sigmoid))
            sgt = p.ms("s")
            p.wait("v", sgt)
            p.op("v", lambda v: v.tensor_scalar(
                out=w1_all[:], in0=w0_all[:], scalar1=-1.0, scalar2=1.0,
                op0=OP.mult, op1=OP.add))
            p.ms("v")
            for gi in range(NCHUNK_ALL):
                b, i = gi // NTT, gi % NTT
                p.wait("t", ("v", p.cnt["v"]))
                p.wait("t", ("s", p.cnt["s"]))
                p.op("t", lambda t, gi=gi: t.matmul(
                    pb[3][0:128, 0:8], lhsT=lstrict[:],
                    rhs=ohs_all[:, gi * 8:(gi + 1) * 8], start=True, stop=True))
                p.op("t", lambda t, gi=gi: t.matmul(
                    pb[4][0:128, 0:8], lhsT=onesf[0:1, 0:128],
                    rhs=baserow[0:1, gi * 8:(gi + 1) * 8], start=True, stop=True))
                pft = p.ms("t")
                p.wait("s", pft)
                p.op("s", lambda s: s.copy(out=thr_sb[:], in_=pb[4][0:128, 0:8]))
                p.wait("v", p.ms("s"))
                p.wait("v", pft)
                p.op("v", lambda v: v.tensor_tensor(
                    out=keepm[:], in0=pb[3][0:128, 0:8], in1=thr_sb[:], op=OP.is_lt))
                p.selfsync("v")
                if gi == NCHUNK_ALL - 1 and "keep127" in taps:
                    tap_outs["keep127"] = nc.dram_tensor("tap_keep127", [128, 8], F32,
                                                         kind="ExternalOutput").ap()
                    tap_outs["thr127"] = nc.dram_tensor("tap_thr127", [128, 8], F32,
                                                        kind="ExternalOutput").ap()
                    tap_outs["win127"] = nc.dram_tensor("tap_win127", [128, 8], F32,
                                                        kind="ExternalOutput").ap()
                    p.op("v", lambda v: v.tensor_copy(out=t8b[:], in_=pb[3][0:128, 0:8]))
                    p.wait("q", p.ms("v"))
                    _k1 = p.dma("q", tap_outs["keep127"], keepm[:])
                    _k2 = p.dma("q", tap_outs["thr127"], thr_sb[:])
                    _k3 = p.dma("q", tap_outs["win127"], t8b[:])
                    p.waits("v", [_k1, _k2, _k3])
                    p.waits("s", [_k2])
                p.op("v", lambda v, gi=gi: v.tensor_sub(
                    oh1[:], ohs_all[:, gi * 8:(gi + 1) * 8],
                    oh0_all[:, gi * 8:(gi + 1) * 8]))
                p.op("v", lambda v, gi=gi: v.tensor_mul(
                    t8a[:], oh0_all[:, gi * 8:(gi + 1) * 8], keepm[:]))
                p.selfsync("v")
                p.op("v", lambda v: v.tensor_reduce(
                    out=colsc[:, 2:3], in_=t8a[:], axis=mybir.AxisListType.X, op=OP.add))
                p.selfsync("v")
                p.op("v", lambda v: v.tensor_mul(t8a[:], oh1[:], keepm[:]))
                p.selfsync("v")
                p.op("v", lambda v: v.tensor_reduce(
                    out=colsc[:, 3:4], in_=t8a[:], axis=mybir.AxisListType.X, op=OP.add))
                p.selfsync("v")
                p.op("v", lambda v, gi=gi: v.tensor_mul(
                    colsc[:, 4:5], w0_all[:, gi:gi + 1], colsc[:, 2:3]))
                p.op("v", lambda v, gi=gi: v.tensor_mul(
                    colsc[:, 5:6], w1_all[:, gi:gi + 1], colsc[:, 3:4]))
                p.selfsync("v")
                p.op("v", lambda v, gi=gi: v.tensor_scalar(
                    out=t8a[:], in0=oh0_all[:, gi * 8:(gi + 1) * 8],
                    scalar1=colsc[:, 4:5], scalar2=None, op0=OP.mult))
                p.selfsync("v")
                p.op("v", lambda v: v.tensor_scalar(
                    out=t8b[:], in0=oh1[:], scalar1=colsc[:, 5:6], scalar2=None,
                    op0=OP.mult))
                p.selfsync("v")
                p.op("v", lambda v: v.tensor_add(t8a[:], t8a[:], t8b[:]))
                p.selfsync("v")
                p.op("v", lambda v, b=b: v.tensor_scalar(
                    out=t8b[:], in0=t8a[:], scalar1=omb[:, b:b + 1], scalar2=None,
                    op0=OP.mult))
                p.selfsync("v")
                if b == 0:
                    p.op("v", lambda v, i=i: v.tensor_copy(
                        out=w_own[:, i * 8:(i + 1) * 8], in_=t8b[:]))
                else:
                    p.op("v", lambda v, i=i: v.tensor_add(
                        w_own[:, i * 8:(i + 1) * 8], w_own[:, i * 8:(i + 1) * 8],
                        t8b[:]))
                p.ms("v")
            p.barrier()
            if "w_own" in taps:
                tap_outs["w_own"] = nc.dram_tensor("tap_w_own", [128, 128], F32,
                                                   kind="ExternalOutput").ap()
                p.dma("q", tap_outs["w_own"], w_own[:])
            if phase == "route":
                raise StopPhase

            for ti in range(NTT):
                p.op("t", lambda t, ti=ti: t.transpose(
                    pb[2][0:8, 0:128], in_=w_own[:, ti * 8:(ti + 1) * 8],
                    identity=identf[:]))
                tk = p.ms("t")
                p.wait("s", tk)
                p.op("s", lambda s, ti=ti: s.copy(
                    out=wr_bf[0:8, ti * 128:(ti + 1) * 128], in_=pb[2][0:8, 0:128]))
                p.wait("t", p.ms("s"))
            if "wr" in taps:
                tap_outs["wr"] = nc.dram_tensor("tap_wr", [8, T], BF16,
                                                kind="ExternalOutput").ap()
                p.wait("q", p.ms("s"))
                p.dma("q", tap_outs["wr"], wr_bf[:])

            rstack.close()
            w1_bf = sbm("w1_bf", [128, NCH * 2048], BF16)
            w2_bf = sbm("w2_bf", [128, 8 * 1024], BF16)
            h_bf = sbm("h_bf", [128, 16 * 256], BF16)
            b1c = sbm("b1c", [128, 16])
            b2all = sbm("b2all", [8, 1024], BF16)
            wbsb = sqs[3]
            p.wait("g", ("t", p.cnt["t"]))
            p.wait("g", ("v", p.cnt["v"]))
            p.wait("g", ("s", p.cnt["s"]))
            ldb2 = p.dma("g", b2all[:], wd["L0_ffn_b2"])
            p.waits("t", [ldb2])
            NT8 = T // 256
            for e in range(8):
                p.wait("g", ("t", p.cnt["t"]))
                lw1 = p.dma("g", w1_bf[:].rearrange("p (dc f) -> p dc f", dc=NCH),
                            wd["L0_ffn_w1"][e].rearrange("(dc pp) f -> pp dc f", pp=128),
                            ch=0)
                lb1 = p.dma("g", b1c[:], wd["L0_ffn_b1"][e].rearrange(
                    "(ncc pp) -> pp ncc", pp=128), ch=2, slow=True)
                p.waits("t", [lw1, lb1])
                for tt in range(NT8):
                    t0 = tt * 256
                    p.op("t", lambda t, e=e, t0=t0: t.matmul(
                        pb[6][0:128, 0:256], lhsT=esel[0:8, e * 128:(e + 1) * 128],
                        rhs=wr_bf[0:8, t0:t0 + 256], start=True, stop=True))
                    wbt = p.ms("t")
                    p.wait("s", wbt)
                    p.op("s", lambda s: s.copy(out=wbsb[:, 0:256], in_=pb[6][0:128, 0:256]))
                    wbst = p.ms("s")
                    p.wait("t", wbst)
                    for ncc in range(16):
                        bank = pb[4 + ncc % 2]
                        for dc in range(NCH):
                            p.op("t", lambda t, ncc=ncc, dc=dc, t0=t0, bank=bank: t.matmul(
                                bank[:, 0:256],
                                lhsT=w1_bf[:, dc * 2048 + ncc * 128: dc * 2048 + (ncc + 1) * 128],
                                rhs=xn[:, dc * T + t0: dc * T + t0 + 256],
                                start=(dc == 0), stop=(dc == NCH - 1)))
                        ht = p.ms("t")
                        p.wait("s", ht)
                        p.op("s", lambda s, ncc=ncc, bank=bank: s.activation(
                            out=tscr[:, 0:256], in_=bank[:, 0:256], func=AF.Gelu,
                            bias=b1c[:, ncc:ncc + 1]))
                        gt = p.ms("s")
                        p.wait("v", gt)
                        p.op("v", lambda v, ncc=ncc: v.tensor_mul(
                            h_bf[:, ncc * 256:(ncc + 1) * 256], tscr[:, 0:256],
                            wbsb[:, 0:256]))
                        p.wait("t", p.ms("v"))
                        p.wait("s", p.ms("v"))
                    if e == 0 and tt == 0 and "hbf0" in taps:
                        tap_outs["hbf0"] = nc.dram_tensor(
                            "tap_hbf0", [128, 16 * 256], BF16, kind="ExternalOutput").ap()
                        tap_outs["wb0"] = nc.dram_tensor(
                            "tap_wb0", [128, 256], BF16, kind="ExternalOutput").ap()
                        p.wait("q", p.ms("v"))
                        p.wait("q", p.ms("s"))
                        _h1 = p.dma("q", tap_outs["hbf0"], h_bf[:])
                        _h2 = p.dma("q", tap_outs["wb0"], wbsb[:, 0:256])
                        p.waits("v", [_h1, _h2])
                        p.waits("s", [_h2])
                        p.waits("t", [_h1, _h2])
                    for half in range(2):
                        p.wait("g", ("t", p.cnt["t"]))
                        lw2 = p.dma("g", w2_bf[:].rearrange("p (hc o) -> p hc o", hc=8),
                                    wd["L0_ffn_w2"][e][half * 1024:(half + 1) * 1024, :].rearrange(
                                        "(hc pp) o -> pp hc o", pp=128), ch=1)
                        p.waits("t", [lw2])
                        for oc in range(NCH):
                            slot = pb[oc // 2][:, (oc % 2) * 256:(oc % 2) * 256 + 256]
                            if e == 0 and half == 0:
                                p.op("t", lambda t, oc=oc, t0=t0, slot=slot: t.matmul(
                                    slot, lhsT=b2all[0:8, oc * 128:(oc + 1) * 128],
                                    rhs=wr_bf[0:8, t0:t0 + 256],
                                    start=True, stop=False, skip_group_check=True))
                            for hc in range(8):
                                ghc = half * 8 + hc
                                p.op("t", lambda t, oc=oc, hc=hc, ghc=ghc, slot=slot, e=e, half=half: t.matmul(
                                    slot,
                                    lhsT=w2_bf[:, hc * 1024 + oc * 128: hc * 1024 + (oc + 1) * 128],
                                    rhs=h_bf[:, ghc * 256:(ghc + 1) * 256],
                                    start=(hc == 0 and not (e == 0 and half == 0)),
                                    stop=(hc == 7),
                                    skip_group_check=True))
                        ot = p.ms("t")
                        p.wait("v", ot)
                        if e == 0 and tt == 0 and half == 0 and "slot0" in taps:
                            tap_outs["slot0"] = nc.dram_tensor(
                                "tap_slot0", [128, 256], F32, kind="ExternalOutput").ap()
                            p.op("v", lambda v: v.tensor_copy(
                                out=tscr[:, 256:512], in_=pb[0][:, 0:256]))
                            p.wait("q", p.ms("v"))
                            _s0 = p.dma("q", tap_outs["slot0"], tscr[:, 256:512])
                            p.waits("v", [_s0])
                            p.waits("s", [_s0])
                        for oc in range(NCH):
                            slot = pb[oc // 2][:, (oc % 2) * 256:(oc % 2) * 256 + 256]
                            p.op("v", lambda v, oc=oc, t0=t0, slot=slot: v.tensor_add(
                                x[:, oc * T + t0: oc * T + t0 + 256],
                                x[:, oc * T + t0: oc * T + t0 + 256], slot))
                        p.wait("t", p.ms("v"))
            p.barrier()
            if "x_moe" in taps:
                tap_outs["x_moe"] = nc.dram_tensor("tap_x_moe", [128, T], F32,
                                                   kind="ExternalOutput").ap()
                p.dma("q", tap_outs["x_moe"], x[:, 0:T])
            moest.close()
            if phase == "moe":
                raise StopPhase

            rstd2tok = p.ms("v")

            gcols = sb0("gcols", [128, 8])
            bcols = sb0("bcols", [128, 8])
            g1 = p.dma("q", gcols[:], wd["L0_attn_gamma"].rearrange("(c p) -> p c", p=128), slow=True)
            g2 = p.dma("q", bcols[:], wd["L0_attn_beta"].rearrange("(c p) -> p c", p=128), slow=True)

            xn2 = xn
            p.waits("v", [g1, g2, rstd2tok])
            for tt in range(NT4):
                bt = bcast_row(rC[0:1, tt * 512:(tt + 1) * 512], 128, 512, pb[6])
                p.wait("v", bt)
                for c in range(NCH):
                    sl = slice(tt * 512, (tt + 1) * 512)
                    p.op("v", lambda v, c=c, sl=sl: v.tensor_mul(
                        tscr[:, :], xn[:, c * T:(c + 1) * T][:, sl], pb[6][:, 0:512]))
                    p.op("v", lambda v, c=c, sl=sl: v.tensor_scalar(
                        out=xn2[:, c * T:(c + 1) * T][:, sl], in0=tscr[:, :],
                        scalar1=gcols[:, c:c + 1], scalar2=bcols[:, c:c + 1],
                        op0=OP.mult, op1=OP.add))
                p.wait("t", p.ms("v"))
            ap_eng[0] = "v"
            tap("xn2", xn2[:, 0:T], [128, T], BF16)
            if phase == "xn2":
                raise StopPhase

            wqkvT = sb0("wqkvT", [128, NCH * 3 * D], BF16)
            wnat = [sb0(f"wnat{i}", [128, D], BF16) for i in range(2)]
            wtr_toks = {}
            for ot in range(3 * D // 128):
                buf = wnat[ot % 2]
                if ot >= 2:
                    p.wait("g", wtr_toks[ot - 2])
                ldt = p.dma("g", buf[:], wd["L0_attn_w_qkv"][ot * 128:(ot + 1) * 128, :])
                p.wait("q", ldt)
                wtr_toks[ot] = p.dma_tr(
                    "q",
                    wqkvT[:].rearrange("p (c o) -> p c o", c=NCH)[:, :, ot * 128:(ot + 1) * 128],
                    buf[:].rearrange("o (c e) -> o c e", c=NCH))
            wqkv_toks = p.dma_tokens("q")
            if phase == "wqkvt":
                p.waits("v", wqkv_toks)
                raise StopPhase

            qp = sb0("qp", [128, T], BF16)
            kp = sb0("kp", [128, T], BF16)
            vtm = sb0("vtm", [128, NTT * 130], BF16)
            expt = [sqs[0], sqs[1]]
            expm = sqs[2]
            rl_sb = tscr
            o_sb = sb0("o_sb", [128, 512])

            p.waits("t", wqkv_toks)
            for hp in range(8):
                for (dst, base, scale) in ((qp, hp * 128, True), (kp, D + hp * 128, False)):
                    for tt in range(NT4):
                        bank = pb[4 + tt % 2]
                        for ec in range(NCH):
                            p.op("t", lambda t, dst=dst, base=base, tt=tt, ec=ec, bank=bank:
                                 t.matmul(bank[:, :],
                                          lhsT=wqkvT[:, ec * 3072 + base: ec * 3072 + base + 128],
                                          rhs=xn2[:, ec * T + tt * 512: ec * T + (tt + 1) * 512],
                                          start=(ec == 0), stop=(ec == NCH - 1)))
                        mt = p.ms("t")
                        p.wait("s", mt)
                        if scale:
                            p.op("s", lambda s, dst=dst, tt=tt, bank=bank: s.activation(
                                out=dst[:, tt * 512:(tt + 1) * 512], in_=bank[:, :],
                                func=AF.Copy, scale=cp("qscale")))
                        else:
                            p.op("s", lambda s, dst=dst, tt=tt, bank=bank: s.copy(
                                out=dst[:, tt * 512:(tt + 1) * 512], in_=bank[:, :]))
                        p.wait("t", p.ms("s"))
                if phase == "proj1" and hp == 0:
                    raise StopPhase
                p.wait("v", ("t", p.cnt["t"]))
                p.op("v", lambda v: v.memset(vtm[:], 1.0))
                p.wait("t", p.ms("v"))
                for ti in range(NTT):
                    bank = pb[6]
                    for ec in range(NCH):
                        p.op("t", lambda t, ti=ti, ec=ec, hp=hp, bank=bank: t.matmul(
                            bank[:, 0:128],
                            lhsT=xn2[:, ec * T + ti * 128: ec * T + (ti + 1) * 128],
                            rhs=wqkvT[:, ec * 3072 + 2 * D + hp * 128: ec * 3072 + 2 * D + (hp + 1) * 128],
                            start=(ec == 0), stop=(ec == NCH - 1)))
                    mt = p.ms("t")
                    p.wait("v", mt)
                    p.op("v", lambda v, ti=ti, bank=bank: v.tensor_copy(
                        out=vtm[:, ti * 130: ti * 130 + 64], in_=bank[:, 0:64]))
                    p.op("v", lambda v, ti=ti, bank=bank: v.tensor_copy(
                        out=vtm[:, ti * 130 + 65: ti * 130 + 129], in_=bank[:, 64:128]))
                    p.wait("t", p.ms("v"))
                if hp == 0:
                    ap_eng[0] = "s"
                    tap("q0", qp[:, 0:T], [128, T], BF16)
                    ap_eng[0] = "v"
                    tap("v0", vtm[:, 0:130 * 4], [128, 520], BF16)
                    if phase == "proj":
                        raise StopPhase

                for hd in range(2):
                    hb = 64 * hd
                    for qt in range(NT4):
                        jmax = 4 * qt + 3
                        ov_toks = {}
                        for j in range(jmax + 1):
                            scb = pb[j % 2]
                            p.op("t", lambda t, j=j, qt=qt, hb=hb, scb=scb: t.matmul(
                                scb[:, :],
                                lhsT=kp[hb:hb + 64, j * 128:(j + 1) * 128],
                                rhs=qp[hb:hb + 64, qt * 512:(qt + 1) * 512],
                                start=True, stop=True))
                            mt = p.ms("t")
                            p.wait("s", mt)
                            eb = expt[j % 2]
                            diag = j >= 4 * qt
                            if j >= 2:
                                p.wait("s", ov_toks[j - 2])
                            if diag:
                                p.op("s", lambda s, scb=scb: s.activation(
                                    out=expm[:, :], in_=scb[:, :], func=AF.Exp, bias=cp("shift")))
                                st = p.ms("s")
                                jj = j - 4 * qt
                                p.wait("v", st)
                                if j >= 2:
                                    p.wait("v", ov_toks[j - 2])
                                p.op("v", lambda v, eb=eb, jj=jj: v.tensor_mul(
                                    eb[:, :], expm[:, :], mext[:, 384 - 128 * jj: 896 - 128 * jj]))
                                st = p.ms("v")
                            else:
                                p.op("s", lambda s, eb=eb, scb=scb: s.activation(
                                    out=eb[:, :], in_=scb[:, :], func=AF.Exp, bias=cp("shift")))
                                st = p.ms("s")
                            p.wait("t", st)
                            p.op("t", lambda t, j=j, hd=hd, eb=eb, jmax=jmax: t.matmul(
                                pb[2][0:65, :],
                                lhsT=vtm[:, j * 130 + 65 * hd: j * 130 + 65 * hd + 65],
                                rhs=eb[:, :], start=(j == 0), stop=(j == jmax)))
                            ov_toks[j] = p.ms("t")
                        ttok = p.ms("t")
                        p.wait("v", ttok)
                        p.op("v", lambda v: v.tensor_copy(
                            out=rL[0:1, :], in_=pb[2][64:65, :]))
                        p.op("v", lambda v: v.reciprocal(rL[0:1, :], rL[0:1, :]))
                        vt = p.ms("v")
                        p.wait("t", vt)
                        bt = bcast_row(rL[0:1, :], 64, 512, pb[3])
                        p.wait("s", bt)
                        p.op("s", lambda s: s.copy(out=rl_sb[0:64, :], in_=pb[3][0:64, :]))
                        stok2 = p.ms("s")
                        p.wait("v", stok2)
                        p.wait("t", stok2)
                        h = 2 * hp + hd
                        ch, pbase = h // 2, 64 * (h % 2)
                        p.op("v", lambda v, pbase=pbase: v.tensor_mul(
                            o_sb[pbase:pbase + 64, :], pb[2][0:64, :], rl_sb[0:64, :]))
                        p.op("v", lambda v, ch=ch, pbase=pbase, qt=qt: v.tensor_add(
                            x[pbase:pbase + 64, ch * T + qt * 512: ch * T + (qt + 1) * 512],
                            x[pbase:pbase + 64, ch * T + qt * 512: ch * T + (qt + 1) * 512],
                            o_sb[pbase:pbase + 64, :]))
                        p.wait("t", p.ms("v"))
                if hp == 0:
                    ap_eng[0] = "v"
                    tap("attn_h0", x[0:64, 0:T], [64, T], F32)
                    if phase == "attn1":
                        raise StopPhase
            p.barrier()
            tap("x_after_attn", x[:, 0:T], [128, T], F32)
            l0.close()

            if nlayers >= 2:
                p.barrier()
                l1 = ExitStack()
                sb1 = lambda n, s, d=F32: l1.enter_context(nc.sbuf_tensor(n, s, d))
                sqs = [sb1(f"l1sq{i}", [128, 512], BF16) for i in range(4)]
                ln_stats(lambda c: x[:, c * T:(c + 1) * T], rA, rB, sqs)
                ln_apply(lambda c: x[:, c * T:(c + 1) * T], rA, rB,
                         lambda c: xn[:, c * T:(c + 1) * T])
                p.barrier()
                if "l1xn" in taps:
                    tap_outs["l1xn"] = nc.dram_tensor("tap_l1xn", [128, T], BF16,
                                                      kind="ExternalOutput").ap()
                    p.dma("q", tap_outs["l1xn"], xn[:, 0:T])

                f1 = ExitStack()
                sbf = lambda n, s, d=F32: f1.enter_context(nc.sbuf_tensor(n, s, d))
                rw1 = sbf("rw1", [128, 64], BF16)
                rb1 = sbf("rb1", [128, 8])
                mwall = sbf("mwall", [128, NTT * 8])
                mwr = sbf("mwr", [8, T], BF16)
                lgs1 = sbf("lgs1", [128, 8]); m81 = sbf("m81", [128, 8])
                nm0 = sbf("nm0", [128, 1]); p81 = sbf("p81", [128, 8])
                den1 = sbf("den1", [128, 1]); msk1 = sbf("msk1", [128, 8])
                t81 = sbf("t81", [128, 8])
                ld1 = p.dma("g", rw1[:].rearrange("p (dc e) -> p dc e", dc=NCH),
                            wd["L1_ffn_router_w"].rearrange("(dc p) e -> p dc e", p=128))
                ld2 = p.dma("q", rb1[:], wd["L1_ffn_router_b"].to_broadcast((128, 8)))
                p.waits("t", [ld1]); p.waits("v", [ld2])
                for ci in range(NTT):
                    p.wait("t", ("v", p.cnt["v"]))
                    for dc in range(NCH):
                        p.op("t", lambda t, ci=ci, dc=dc: t.matmul(
                            pb[2][0:128, 0:8],
                            lhsT=xn[:, dc * T + ci * 128: dc * T + (ci + 1) * 128],
                            rhs=rw1[:, dc * 8:(dc + 1) * 8],
                            start=(dc == 0), stop=(dc == NCH - 1)))
                    lt = p.ms("t")
                    p.wait("v", lt)
                    p.op("v", lambda v: v.tensor_add(lgs1[:], pb[2][0:128, 0:8], rb1[:]))
                    p.selfsync("v")
                    p.op("v", lambda v: v.max(m81[:], lgs1[:]))
                    p.selfsync("v")
                    p.op("v", lambda v: v.tensor_scalar_mul(nm0[:], m81[:, 0:1], -1.0))
                    p.wait("s", p.ms("v"))
                    p.op("s", lambda s: s.activation(out=p81[:], in_=lgs1[:],
                                                     func=AF.Exp, bias=nm0[:, 0:1]))
                    p.wait("v", p.ms("s"))
                    p.op("v", lambda v: v.tensor_reduce(
                        out=den1[:], in_=p81[:], axis=mybir.AxisListType.X, op=OP.add))
                    p.selfsync("v")
                    p.op("v", lambda v: v.reciprocal(den1[:], den1[:]))
                    p.selfsync("v")
                    p.op("v", lambda v: v.tensor_scalar(
                        out=msk1[:], in0=lgs1[:], scalar1=m81[:, 0:1], scalar2=None,
                        op0=OP.is_equal))
                    p.selfsync("v")
                    p.op("v", lambda v: v.tensor_scalar(
                        out=t81[:], in0=lgs1[:], scalar1=m81[:, 1:2], scalar2=None,
                        op0=OP.is_equal))
                    p.selfsync("v")
                    p.op("v", lambda v: v.tensor_add(msk1[:], msk1[:], t81[:]))
                    p.op("v", lambda v: v.tensor_mul(t81[:], p81[:], msk1[:]))
                    p.selfsync("v")
                    p.op("v", lambda v, ci=ci: v.tensor_scalar(
                        out=mwall[:, ci * 8:(ci + 1) * 8], in0=t81[:],
                        scalar1=den1[:, 0:1], scalar2=None, op0=OP.mult))
                    p.ms("v")
                p.barrier()
                if "l1mw" in taps:
                    tap_outs["l1mw"] = nc.dram_tensor("tap_l1mw", [128, NTT * 8], F32,
                                                      kind="ExternalOutput").ap()
                    p.dma("q", tap_outs["l1mw"], mwall[:])
                for ti in range(NTT):
                    p.op("t", lambda t, ti=ti: t.transpose(
                        pb[2][0:8, 0:128], in_=mwall[:, ti * 8:(ti + 1) * 8],
                        identity=identf[:]))
                    tk = p.ms("t")
                    p.wait("s", tk)
                    p.op("s", lambda s, ti=ti: s.copy(
                        out=mwr[0:8, ti * 128:(ti + 1) * 128], in_=pb[2][0:8, 0:128]))
                    p.wait("t", p.ms("s"))
                LTT = 256
                w1sb = sbf("w1sb", [128, NCH * 256], BF16)
                u_sb = sbf("u_sb", [128, 8 * 2 * LTT], BF16)
                h1bf = sbf("h1bf", [128, 32 * LTT], BF16)
                w2sb = sbf("w2sb", [128, 16 * 128], BF16)
                l2wsb = sbf("l2wsb", [128, 32 * 128], BF16)
                l2bc = sbf("l2bc", [128, 8])
                mwb = sbf("mwb", [128, LTT])
                ldb = p.dma("q", l2bc[:], wd["L1_ffn_lin2_b"].rearrange(
                    "(c p) -> p c", p=128), slow=True)
                p.waits("v", [ldb])
                for tt in range(T // LTT):
                    t0 = tt * LTT
                    for e in range(8):
                        p.wait("g", ("t", p.cnt["t"]))
                        lw = p.dma("g", w1sb[:].rearrange("p (dc r) -> p dc r", dc=NCH),
                                   wd["L1_ffn_w1"][e].rearrange("(dc p) r -> p dc r", p=128),
                                   ch=0)
                        p.waits("t", [lw])
                        p.wait("t", ("s", p.cnt["s"]))
                        p.op("t", lambda t, e=e, t0=t0: t.matmul(
                            pb[6][:, 0:LTT], lhsT=esel[0:8, e * 128:(e + 1) * 128],
                            rhs=mwr[0:8, t0:t0 + LTT], start=True, stop=True))
                        bt = p.ms("t")
                        p.wait("s", bt)
                        p.op("s", lambda s: s.copy(out=mwb[:], in_=pb[6][:, 0:LTT]))
                        p.wait("v", p.ms("s"))
                        for rc in range(2):
                            p.wait("t", ("v", p.cnt["v"]))
                            for dc in range(NCH):
                                p.op("t", lambda t, dc=dc, rc=rc, t0=t0: t.matmul(
                                    pb[rc][:, 0:LTT],
                                    lhsT=w1sb[:, dc * 256 + rc * 128: dc * 256 + (rc + 1) * 128],
                                    rhs=xn[:, dc * T + t0: dc * T + t0 + LTT],
                                    start=(dc == 0), stop=(dc == NCH - 1)))
                            ut = p.ms("t")
                            p.wait("v", ut)
                            p.op("v", lambda v, e=e, rc=rc: v.tensor_mul(
                                u_sb[:, (e * 2 + rc) * LTT:(e * 2 + rc + 1) * LTT],
                                pb[rc][:, 0:LTT], mwb[:]))
                            p.ms("v")
                    for fc in range(32):
                        p.wait("g", ("t", p.cnt["t"]))
                        lw2 = p.dma("g", w2sb[:].rearrange("p (er f) -> p er f", er=16),
                                    wd["L1_ffn_w2"][:, :, fc * 128:(fc + 1) * 128].rearrange(
                                        "e (rc p) f -> p (e rc) f", p=128), ch=1)
                        p.waits("t", [lw2])
                        p.wait("t", ("v", p.cnt["v"]))
                        p.wait("t", ("s", p.cnt["s"]))
                        for e in range(8):
                            for rc in range(2):
                                er = e * 2 + rc
                                p.op("t", lambda t, er=er: t.matmul(
                                    pb[4][:, 0:LTT],
                                    lhsT=w2sb[:, er * 128:(er + 1) * 128],
                                    rhs=u_sb[:, er * LTT:(er + 1) * LTT],
                                    start=(er == 0), stop=(er == 15)))
                        mt = p.ms("t")
                        p.wait("s", mt)
                        p.op("s", lambda s, fc=fc: s.activation(
                            out=h1bf[:, fc * LTT:(fc + 1) * LTT], in_=pb[4][:, 0:LTT],
                            func=AF.Gelu))
                        p.ms("s")
                    p.wait("t", ("s", p.cnt["s"]))
                    for oc in range(NCH):
                        p.wait("g", ("t", p.cnt["t"]))
                        lw3 = p.dma("g", l2wsb[:].rearrange("p (fc o) -> p fc o", fc=32),
                                    wd["L1_ffn_lin2_w"][:, oc * 128:(oc + 1) * 128].rearrange(
                                        "(fc p) o -> p fc o", p=128), ch=2)
                        p.waits("t", [lw3])
                        p.wait("t", ("v", p.cnt["v"]))
                        for fc in range(32):
                            p.op("t", lambda t, fc=fc: t.matmul(
                                pb[5][:, 0:LTT], lhsT=l2wsb[:, fc * 128:(fc + 1) * 128],
                                rhs=h1bf[:, fc * LTT:(fc + 1) * LTT],
                                start=(fc == 0), stop=(fc == 31)))
                        ot = p.ms("t")
                        p.wait("v", ot)
                        p.op("v", lambda v, oc=oc: v.tensor_scalar(
                            out=tscr[:, 0:LTT], in0=pb[5][:, 0:LTT], scalar1=l2bc[:, oc:oc + 1],
                            scalar2=None, op0=OP.add))
                        p.selfsync("v")
                        p.op("v", lambda v, oc=oc, t0=t0: v.tensor_add(
                            x[:, oc * T + t0: oc * T + t0 + LTT],
                            x[:, oc * T + t0: oc * T + t0 + LTT], tscr[:, 0:LTT]))
                        p.ms("v")
                p.barrier()
                f1.close()
                if "l1ffn" in taps:
                    tap_outs["l1ffn"] = nc.dram_tensor("tap_l1ffn", [128, T], F32,
                                                       kind="ExternalOutput").ap()
                    p.dma("q", tap_outs["l1ffn"], x[:, 0:T])
                if phase == "l1ffn":
                    raise StopPhase

                a1 = ExitStack()
                sba = lambda n, s, d=F32: a1.enter_context(nc.sbuf_tensor(n, s, d))
                wqf = sba("wqf", [128, 256], BF16)
                wkf = sba("wkf", [128, 256], BF16)
                bqc = sba("bqc", [128, 8]); bkc = sba("bkc", [128, 8])
                bpc = sba("bpc", [128, 8])
                bvrow = sba("bvrow", [1, D], BF16)
                ksb = sba("ksb", [128, 2], BF16)
                ktm_bf = sba("ktm_bf", [128, 256], BF16)
                ctx_bf = sba("ctx_bf", [128, 2 * 64], BF16)
                ksrow = sba("ksrow", [1, 256])
                denr = rC
                p.barrier()
                _l1 = p.dma("g", wqf[0:64, :], wd["L1_attn_wq_feat"], ch=0)
                _l2 = p.dma("g", wqf[64:128, :], wd["L1_attn_wq_feat"], ch=1)
                _l3 = p.dma("g", wkf[0:64, :], wd["L1_attn_wk_feat"], ch=2)
                _l4 = p.dma("g", wkf[64:128, :], wd["L1_attn_wk_feat"], ch=3)
                _l5 = p.dma("g", bvrow[:], wd["L1_attn_bv"], ch=0)
                _l6 = p.dma("q", bqc[:], wd["L1_attn_bq"].rearrange("(c p) -> p c", p=128), slow=True)
                _l7 = p.dma("q", bkc[:], wd["L1_attn_bk"].rearrange("(c p) -> p c", p=128), slow=True)
                _l8 = p.dma("q", bpc[:], wd["L1_attn_bproj"].rearrange("(c p) -> p c", p=128), slow=True)
                p.waits("t", [_l1, _l2, _l3, _l4, _l5])
                p.waits("v", [_l6, _l7, _l8])
                wmat = sba("wmat", [128, NCH * 512], BF16)
                v_tm = sba("v_tm", [128, NTT * 512], BF16)
                qfm = sba("qfm", [128, 4 * T], BF16)
                kfm = sba("kfm", [128, 4 * T], BF16)
                qpr = sba("qpr", [128, 2 * 512], BF16)
                o_bf = sba("o_bf", [128, 512], BF16)
                p.barrier()
                for g in range(2):
                    p.wait("g", ("t", p.cnt["t"]))
                    p.wait("g", ("v", p.cnt["v"]))
                    p.wait("g", ("s", p.cnt["s"]))
                    lw = p.dma("g", wmat[:].rearrange("p (dc o) -> p dc o", dc=NCH),
                               wd["L1_attn_Wv"][:, g * 512:(g + 1) * 512].rearrange(
                                   "(dc p) o -> p dc o", p=128), ch=0)
                    p.waits("t", [lw])
                    for ti in range(NTT):
                        p.wait("t", ("v", p.cnt["v"]))
                        p.op("t", lambda t, g=g: t.matmul(
                            pb[6][:, :], lhsT=onesb[0:1, 0:128],
                            rhs=bvrow[0:1, g * 512:(g + 1) * 512],
                            start=True, stop=False))
                        for dc in range(NCH):
                            p.op("t", lambda t, ti=ti, dc=dc: t.matmul(
                                pb[6][:, :],
                                lhsT=xn[:, dc * T + ti * 128: dc * T + (ti + 1) * 128],
                                rhs=wmat[:, dc * 512:(dc + 1) * 512],
                                start=False, stop=(dc == NCH - 1)))
                        vt = p.ms("t")
                        p.wait("v", vt)
                        p.op("v", lambda v, ti=ti: v.tensor_copy(
                            out=v_tm[:, ti * 512:(ti + 1) * 512], in_=pb[6][:, :]))
                        p.ms("v")
                    for (dst, wname, bcol) in ((qfm, "L1_attn_Wq", bqc), (kfm, "L1_attn_Wk", bkc)):
                        p.wait("g", ("t", p.cnt["t"]))
                        lw = p.dma("g", wmat[:].rearrange("p (dc o) -> p dc o", dc=NCH),
                                   wd[wname][:, g * 512:(g + 1) * 512].rearrange(
                                       "(dc p) o -> p dc o", p=128), ch=0)
                        p.waits("t", [lw])
                        for lc in range(4):
                            for tt in range(NT4):
                                p.wait("t", ("v", p.cnt["v"]))
                                for dc in range(NCH):
                                    p.op("t", lambda t, lc=lc, tt=tt, dc=dc: t.matmul(
                                        pb[4 + tt % 2][:, :],
                                        lhsT=wmat[:, dc * 512 + lc * 128: dc * 512 + (lc + 1) * 128],
                                        rhs=xn[:, dc * T + tt * 512: dc * T + (tt + 1) * 512],
                                        start=(dc == 0), stop=(dc == NCH - 1)))
                                qt_ = p.ms("t")
                                p.wait("v", qt_)
                                p.op("v", lambda v, dst=dst, lc=lc, tt=tt, g=g, bcol=bcol: v.tensor_scalar(
                                    out=dst[:, lc * T + tt * 512: lc * T + (tt + 1) * 512],
                                    in0=pb[4 + tt % 2][:, :],
                                    scalar1=bcol[:, (g * 4 + lc):(g * 4 + lc) + 1],
                                    scalar2=None, op0=OP.add))
                                p.ms("v")
                    p.barrier()
                    p.wait("g", ("t", p.cnt["t"]))
                    lwp = p.dma("g", wmat[:].rearrange("p (c o) -> p c o", c=4),
                                wd["L1_attn_Wproj"][g * 512:(g + 1) * 512, :].rearrange(
                                    "(c p) o -> p c o", p=128), ch=0)
                    p.waits("t", [lwp])
                    for hloc in range(8):
                        lc, hb = hloc // 2, 64 * (hloc % 2)
                        for ti in range(NTT):
                            p.op("t", lambda t, lc=lc, hb=hb, ti=ti: t.matmul(
                                pb[6][0:128, 0:256],
                                lhsT=kfm[hb:hb + 64, lc * T + ti * 128: lc * T + (ti + 1) * 128],
                                rhs=wkf[hb:hb + 64, :], start=True, stop=True))
                            kt = p.ms("t")
                            p.wait("s", kt)
                            if ti >= 1:
                                p.wait("s", ("t", p.cnt["t"]))
                            p.op("s", lambda s: s.activation(
                                out=ktm_bf[:], in_=pb[6][0:128, 0:256], func=AF.Exp))
                            st = p.ms("s")
                            p.wait("t", st)
                            p.op("t", lambda t, ti=ti: t.matmul(
                                pb[5][0:1, 0:256], lhsT=onesb[:, 0:1], rhs=ktm_bf[:],
                                start=(ti == 0), stop=(ti == NTT - 1),
                                skip_group_check=True))
                            for fc in range(2):
                                p.op("t", lambda t, fc=fc, ti=ti, hloc=hloc: t.matmul(
                                    pb[3 + fc][0:128, 0:64],
                                    lhsT=ktm_bf[:, fc * 128:(fc + 1) * 128],
                                    rhs=v_tm[:, ti * 512 + hloc * 64: ti * 512 + hloc * 64 + 64],
                                    start=(ti == 0), stop=(ti == NTT - 1),
                                    skip_group_check=True))
                            p.ms("t")
                        tk = p.ms("t")
                        p.wait("v", tk)
                        p.op("v", lambda v: v.tensor_copy(out=ksrow[:], in_=pb[5][0:1, 0:256]))
                        p.op("v", lambda v: v.tensor_copy(out=ctx_bf[:, 0:64], in_=pb[3][0:128, 0:64]))
                        p.op("v", lambda v: v.tensor_copy(out=ctx_bf[:, 64:128], in_=pb[4][0:128, 0:64]))
                        kvt = p.ms("v")
                        p.wait("q", kvt)
                        p.dma_quiesce("q")
                        b1_ = p.dma("q", rsc_d[0:1, 0:256], ksrow[:])
                        p.waits("g", [b1_])
                        b2_ = p.dma("g", ksb[:], rsc_d[0:1, 0:256].rearrange(
                            "o (fc p) -> (o p) fc", p=128), slow=True)
                        p.waits("t", [b2_])
                        for tt in range(NT4):
                            p.wait("t", ("s", p.cnt["s"]))
                            for fc in range(2):
                                p.op("t", lambda t, fc=fc, lc=lc, hb=hb, tt=tt: t.matmul(
                                    pb[6 + fc][:, :],
                                    lhsT=wqf[hb:hb + 64, fc * 128:(fc + 1) * 128],
                                    rhs=qfm[hb:hb + 64, lc * T + tt * 512: lc * T + (tt + 1) * 512],
                                    start=True, stop=True))
                            qt_ = p.ms("t")
                            p.wait("s", qt_)
                            for fc in range(2):
                                p.op("s", lambda s, fc=fc: s.activation(
                                    out=qpr[:, fc * 512:(fc + 1) * 512],
                                    in_=pb[6 + fc][:, :], func=AF.Exp))
                            p.wait("t", p.ms("s"))
                            for fc in range(2):
                                p.op("t", lambda t, fc=fc: t.matmul(
                                    pb[2][0:1, :], lhsT=ksb[:, fc:fc + 1],
                                    rhs=qpr[:, fc * 512:(fc + 1) * 512],
                                    start=(fc == 0), stop=(fc == 1)))
                            dt_ = p.ms("t")
                            p.wait("v", dt_)
                            p.op("v", lambda v: v.tensor_scalar_add(
                                rL[0:1, :], pb[2][0:1, :], 1e-6))
                            p.selfsync("v")
                            p.op("v", lambda v: v.reciprocal(rL[0:1, :], rL[0:1, :]))
                            p.selfsync("v")
                            p.wait("t", p.ms("v"))
                            for fc in range(2):
                                p.op("t", lambda t, fc=fc: t.matmul(
                                    pb[2][0:64, :],
                                    lhsT=ctx_bf[:, fc * 64:(fc + 1) * 64],
                                    rhs=qpr[:, fc * 512:(fc + 1) * 512],
                                    start=(fc == 0), stop=(fc == 1)))
                            bt = bcast_row(rL[0:1, :], 64, 512, pb[3])
                            p.wait("s", bt)
                            p.op("s", lambda s: s.copy(out=tscr[0:64, :], in_=pb[3][0:64, :]))
                            p.wait("v", p.ms("s"))
                            p.op("v", lambda v, hb=hb: v.tensor_mul(
                                o_bf[hb:hb + 64, :], pb[2][0:64, :], tscr[0:64, :]))
                            p.wait("t", p.ms("v"))
                            p.wait("s", p.ms("v"))
                            if g == 0 and hloc == 0 and tt == 0 and "l1h0" in taps:
                                tap_outs["l1h0"] = nc.dram_tensor(
                                    "tap_l1h0", [128, 2048], F32, kind="ExternalOutput").ap()
                                p.wait("q", p.ms("v"))
                                p.wait("g", p.ms("v"))
                                _d1 = p.dma("g", tap_outs["l1h0"][:, 0:512], qpr[:, 0:512], ch=0)
                                _d2 = p.dma("g", tap_outs["l1h0"][:, 512:640], ctx_bf[:], ch=1)
                                _d3 = p.dma("q", tap_outs["l1h0"][0:1, 640:896], ksrow[:])
                                _d4 = p.dma("q", tap_outs["l1h0"][0:1, 896:1408], rL[0:1, :])
                                _d5 = p.dma("g", tap_outs["l1h0"][0:64, 1408:1920], o_bf[0:64, :], ch=2)
                                p.waits("v", [_d1, _d2, _d3, _d4, _d5])
                                p.waits("s", [_d1])
                                p.waits("t", [_d2, _d3, _d4, _d5])
                            for oc in range(NCH):
                                p.wait("t", ("v", p.cnt["v"]))
                                p.op("t", lambda t, hloc=hloc, hb=hb, oc=oc: t.matmul(
                                    pb[4][:, :],
                                    lhsT=wmat[hb:hb + 64, (hloc // 2) * 1024 + oc * 128:
                                              (hloc // 2) * 1024 + (oc + 1) * 128],
                                    rhs=o_bf[hb:hb + 64, :], start=True, stop=True))
                                ot = p.ms("t")
                                p.wait("v", ot)
                                if g == 0 and hloc == 0:
                                    p.op("v", lambda v, oc=oc: v.tensor_scalar(
                                        out=tscr[:, :], in0=pb[4][:, :],
                                        scalar1=bpc[:, oc:oc + 1], scalar2=None, op0=OP.add))
                                    p.selfsync("v")
                                    p.op("v", lambda v, oc=oc, tt=tt: v.tensor_add(
                                        x[:, oc * T + tt * 512: oc * T + (tt + 1) * 512],
                                        x[:, oc * T + tt * 512: oc * T + (tt + 1) * 512],
                                        tscr[:, :]))
                                else:
                                    p.op("v", lambda v, oc=oc, tt=tt: v.tensor_add(
                                        x[:, oc * T + tt * 512: oc * T + (tt + 1) * 512],
                                        x[:, oc * T + tt * 512: oc * T + (tt + 1) * 512],
                                        pb[4][:, :]))
                                p.ms("v")
                    p.barrier()
                a1.close()
                l1.close()
                p.barrier()
                if "l1x" in taps:
                    tap_outs["l1x"] = nc.dram_tensor("tap_l1x", [128, T], F32,
                                                     kind="ExternalOutput").ap()
                    p.dma("q", tap_outs["l1x"], x[:, 0:T])
                if phase == "l1":
                    raise StopPhase

            if nlayers >= 3:
                p.barrier()
                l2 = ExitStack()
                sb2 = lambda n, s, d=F32: l2.enter_context(nc.sbuf_tensor(n, s, d))
                sqs2 = [sb2(f"l2sq{i}", [128, 512], BF16) for i in range(4)]
                ln_stats(lambda c: x[:, c * T:(c + 1) * T], rA, rB, sqs2)
                ln_apply(lambda c: x[:, c * T:(c + 1) * T], rA, rB,
                         lambda c: xn[:, c * T:(c + 1) * T])
                p.barrier()

                def load_wT(dst, wsrc, stage):
                    nout = wsrc.shape[0]
                    for tkq in p.dma_tokens("q"):
                        p.wait("g", tkq)
                    for ot in range(nout // 128):
                        buf = stage[ot % 2]
                        if ot >= 2:
                            p.wait("g", wt_toks[ot - 2])
                        ldt = p.dma("g", buf[:], wsrc[ot * 128:(ot + 1) * 128, :])
                        p.wait("q", ldt)
                        wt_toks[ot] = p.dma_tr(
                            "q", dst[:].rearrange("p (c o) -> p c o", c=NCH)[:, :, ot * 128:(ot + 1) * 128],
                            buf[:].rearrange("o (c e) -> o c e", c=NCH))
                    return p.dma_tokens("q")

                f2 = ExitStack()
                sbf2 = lambda n, s, d=F32: f2.enter_context(nc.sbuf_tensor(n, s, d))
                wrT = sbf2("wrT", [128, NCH * D], BF16)
                wkT = sbf2("wkT", [128, NCH * D], BF16)
                wvT = sbf2("wvT", [128, NCH * D], BF16)
                stage2 = [sbf2(f"l2st{i}", [128, D], BF16) for i in range(2)]
                murc = sbf2("murc", [128, 8]); mukc = sbf2("mukc", [128, 8])
                mur1 = sbf2("mur1", [128, 8]); muk1 = sbf2("muk1", [128, 8])
                xr_t = sbf2("xr_t", [128, NCH * 256], BF16)
                xk_t = sbf2("xk_t", [128, NCH * 256], BF16)
                r_t = sbf2("r_t", [128, NCH * 256], BF16)
                ka_t = sbf2("ka_t", [128, NCH * 256], BF16)
                p.barrier()
                wt_toks = {}
                tk1 = load_wT(wrT, wd["L2_ffn_W_r"], stage2)
                tk2 = load_wT(wkT, wd["L2_ffn_W_k"], stage2)
                tk3 = load_wT(wvT, wd["L2_ffn_W_v"], stage2)
                m1 = p.dma("g", murc[:], wd["L2_ffn_mu_r"].rearrange("(c p) -> p c", p=128), slow=True, ch=0)
                m2 = p.dma("g", mukc[:], wd["L2_ffn_mu_k"].rearrange("(c p) -> p c", p=128), slow=True, ch=1)
                p.waits("v", [m1, m2])
                p.op("v", lambda v: v.tensor_scalar(
                    out=mur1[:], in0=murc[:], scalar1=-1.0, scalar2=1.0,
                    op0=OP.mult, op1=OP.add))
                p.op("v", lambda v: v.tensor_scalar(
                    out=muk1[:], in0=mukc[:], scalar1=-1.0, scalar2=1.0,
                    op0=OP.mult, op1=OP.add))
                p.ms("v")
                p.waits("t", tk1 + tk2 + tk3)
                p.waits("v", tk1 + tk2 + tk3)
                for tt in range(T // 256):
                    t0 = tt * 256
                    p.wait("v", ("t", p.cnt["t"]))
                    for c in range(NCH):
                        if tt == 0:
                            p.op("v", lambda v, c=c: v.memset(tscr[:, 0:1], 0.0))
                            p.op("v", lambda v, c=c: v.tensor_copy(
                                out=tscr[:, 1:256], in_=xn[:, c * T: c * T + 255]))
                        else:
                            p.op("v", lambda v, c=c, t0=t0: v.tensor_copy(
                                out=tscr[:, 0:256], in_=xn[:, c * T + t0 - 1: c * T + t0 + 255]))
                        p.selfsync("v")
                        p.wait("s", p.ms("v"))
                        for (dst, mc, m1c) in ((xr_t, murc, mur1), (xk_t, mukc, muk1)):
                            p.op("s", lambda s, c=c, mc=mc, t0=t0: s.activation(
                                out=sqs2[0][:, 0:256], in_=xn[:, c * T + t0: c * T + t0 + 256],
                                func=AF.Copy, scale=mc[:, c:c + 1]))
                            p.op("s", lambda s, c=c, m1c=m1c: s.activation(
                                out=sqs2[1][:, 0:256], in_=tscr[:, 0:256],
                                func=AF.Copy, scale=m1c[:, c:c + 1]))
                            p.wait("v", p.ms("s"))
                            p.op("v", lambda v, c=c, dst=dst: v.tensor_add(
                                dst[:, c * 256:(c + 1) * 256], sqs2[0][:, 0:256], sqs2[1][:, 0:256]))
                            p.wait("s", p.ms("v"))
                        p.ms("v")
                    p.wait("t", p.ms("v"))
                    for oc in range(NCH):
                        p.wait("t", ("s", p.cnt["s"]))
                        p.wait("t", ("v", p.cnt["v"]))
                        for dc in range(NCH):
                            p.op("t", lambda t, oc=oc, dc=dc: t.matmul(
                                pb[4][:, 0:256],
                                lhsT=wrT[:, dc * D + oc * 128: dc * D + (oc + 1) * 128],
                                rhs=xr_t[:, dc * 256:(dc + 1) * 256],
                                start=(dc == 0), stop=(dc == NCH - 1)))
                        rt_ = p.ms("t")
                        p.wait("s", rt_)
                        p.op("s", lambda s, oc=oc: s.activation(
                            out=r_t[:, oc * 256:(oc + 1) * 256], in_=pb[4][:, 0:256],
                            func=AF.Sigmoid))
                        p.wait("t", p.ms("s"))
                        for dc in range(NCH):
                            p.op("t", lambda t, oc=oc, dc=dc: t.matmul(
                                pb[5][:, 0:256],
                                lhsT=wkT[:, dc * D + oc * 128: dc * D + (oc + 1) * 128],
                                rhs=xk_t[:, dc * 256:(dc + 1) * 256],
                                start=(dc == 0), stop=(dc == NCH - 1)))
                        kt_ = p.ms("t")
                        p.wait("s", kt_)
                        p.op("s", lambda s: s.activation(
                            out=tscr[:, 0:256], in_=pb[5][:, 0:256], func=AF.Relu))
                        st_ = p.ms("s")
                        p.wait("v", st_)
                        p.op("v", lambda v, oc=oc: v.tensor_mul(
                            ka_t[:, oc * 256:(oc + 1) * 256], tscr[:, 0:256], tscr[:, 0:256]))
                        p.wait("t", p.ms("v"))
                        p.wait("s", p.ms("v"))
                    if tt == 0 and "l2fi" in taps:
                        tap_outs["l2fi"] = nc.dram_tensor(
                            "tap_l2fi", [128, 3 * 2048], F32, kind="ExternalOutput").ap()
                        p.wait("g", p.ms("v"))
                        p.wait("g", p.ms("s"))
                        _f1 = p.dma("g", tap_outs["l2fi"][:, 0:2048], xr_t[:], ch=0)
                        _f2 = p.dma("g", tap_outs["l2fi"][:, 2048:4096], r_t[:], ch=1)
                        _f3 = p.dma("g", tap_outs["l2fi"][:, 4096:6144], ka_t[:], ch=2)
                        p.waits("v", [_f1, _f2, _f3])
                        p.waits("s", [_f2])
                        p.waits("t", [_f1, _f3])
                    for oc in range(NCH):
                        p.wait("t", ("v", p.cnt["v"]))
                        for dc in range(NCH):
                            p.op("t", lambda t, oc=oc, dc=dc: t.matmul(
                                pb[6][:, 0:256],
                                lhsT=wvT[:, dc * D + oc * 128: dc * D + (oc + 1) * 128],
                                rhs=ka_t[:, dc * 256:(dc + 1) * 256],
                                start=(dc == 0), stop=(dc == NCH - 1)))
                        vt_ = p.ms("t")
                        p.wait("v", vt_)
                        p.op("v", lambda v, oc=oc: v.tensor_mul(
                            tscr[:, 0:256], pb[6][:, 0:256], r_t[:, oc * 256:(oc + 1) * 256]))
                        p.selfsync("v")
                        p.op("v", lambda v, oc=oc, t0=t0: v.tensor_add(
                            x[:, oc * T + t0: oc * T + t0 + 256],
                            x[:, oc * T + t0: oc * T + t0 + 256], tscr[:, 0:256]))
                        p.wait("t", p.ms("v"))
                p.barrier()
                f2.close()
                if "l2ffn" in taps:
                    tap_outs["l2ffn"] = nc.dram_tensor("tap_l2ffn", [128, T], F32,
                                                       kind="ExternalOutput").ap()
                    p.dma("q", tap_outs["l2ffn"], x[:, 0:T])

                a2 = ExitStack()
                sa2 = lambda n, s, d=F32: a2.enter_context(nc.sbuf_tensor(n, s, d))
                w2T = sa2("w2T", [128, NCH * D], BF16)
                stage3 = [sa2(f"l2as{i}", [128, D], BF16) for i in range(2)]
                bqkc = sa2("bqkc", [128, 8]); bvic = sa2("bvic", [128, 8])
                biqc = sa2("biqc", [128, 8]); bikc = sa2("bikc", [128, 8])
                bivc = sa2("bivc", [128, 8]); boc2 = sa2("boc2", [128, 8])
                qk_t = sa2("qk_t", [128, NCH * 512], BF16)
                vi_t = sa2("vi_t", [128, NCH * 512], BF16)
                q2_t = sa2("q2_t", [128, NCH * 512], BF16)
                k2_t = sa2("k2_t", [128, NCH * 512], BF16)
                v2_t = sa2("v2_t", [128, NCH * 512], BF16)
                o2_t = sa2("o2_t", [128, NCH * 512], BF16)
                pbf = sqs2[0]
                p.barrier()
                bl1 = p.dma("g", bqkc[:], wd["L2_attn_proj_qk_b"].rearrange("(c p) -> p c", p=128), slow=True, ch=0)
                bl2 = p.dma("g", bvic[:], wd["L2_attn_proj_v_b"].rearrange("(c p) -> p c", p=128), slow=True, ch=1)
                bl3 = p.dma("g", biqc[:], wd["L2_attn_in_proj_b"][0:D].rearrange("(c p) -> p c", p=128), slow=True, ch=2)
                bl4 = p.dma("g", bikc[:], wd["L2_attn_in_proj_b"][D:2 * D].rearrange("(c p) -> p c", p=128), slow=True, ch=3)
                bl5 = p.dma("g", bivc[:], wd["L2_attn_in_proj_b"][2 * D:3 * D].rearrange("(c p) -> p c", p=128), slow=True, ch=0)
                bl6 = p.dma("g", boc2[:], wd["L2_attn_out_proj_b"].rearrange("(c p) -> p c", p=128), slow=True, ch=1)
                p.waits("v", [bl1, bl2, bl3, bl4, bl5, bl6])

                def proj_tile(dst, srcT, t0local, bcol, scale=None, src_is_x=False):
                    for oc in range(NCH):
                        for dc in range(NCH):
                            if src_is_x:
                                rhs_ap = xn[:, dc * T + t0local: dc * T + t0local + 512]
                            else:
                                rhs_ap = src_is_x
                            p.op("t", lambda t, oc=oc, dc=dc, rhs_ap=rhs_ap: t.matmul(
                                pb[4][:, :],
                                lhsT=w2T[:, dc * D + oc * 128: dc * D + (oc + 1) * 128],
                                rhs=rhs_ap, start=(dc == 0), stop=(dc == NCH - 1)))
                        pt_ = p.ms("t")
                        p.wait("v", pt_)
                        if scale is None:
                            p.op("v", lambda v, oc=oc: v.tensor_scalar(
                                out=dst[:, oc * 512:(oc + 1) * 512], in0=pb[4][:, :],
                                scalar1=bcol[:, oc:oc + 1], scalar2=None, op0=OP.add))
                        else:
                            p.op("v", lambda v, oc=oc: v.tensor_scalar(
                                out=tscr[:, :], in0=pb[4][:, :],
                                scalar1=bcol[:, oc:oc + 1], scalar2=None, op0=OP.add))
                            p.selfsync("v")
                            p.op("v", lambda v, oc=oc: v.tensor_scalar_mul(
                                dst[:, oc * 512:(oc + 1) * 512], tscr[:, :], scale))
                        p.ms("v")
                    p.wait("t", p.ms("v"))

                def proj_tile2(dst, src, bcol, scale=None):
                    for oc in range(NCH):
                        p.wait("t", ("v", p.cnt["v"]))
                        for dc in range(NCH):
                            p.op("t", lambda t, oc=oc, dc=dc: t.matmul(
                                pb[4][:, :],
                                lhsT=w2T[:, dc * D + oc * 128: dc * D + (oc + 1) * 128],
                                rhs=src[:, dc * 512:(dc + 1) * 512],
                                start=(dc == 0), stop=(dc == NCH - 1)))
                        pt_ = p.ms("t")
                        p.wait("v", pt_)
                        if scale is None:
                            p.op("v", lambda v, oc=oc: v.tensor_scalar(
                                out=dst[:, oc * 512:(oc + 1) * 512], in0=pb[4][:, :],
                                scalar1=bcol[:, oc:oc + 1], scalar2=None, op0=OP.add))
                        else:
                            p.op("v", lambda v, oc=oc: v.tensor_scalar(
                                out=tscr[:, :], in0=pb[4][:, :],
                                scalar1=bcol[:, oc:oc + 1], scalar2=None, op0=OP.add))
                            p.selfsync("v")
                            p.op("v", lambda v, oc=oc: v.tensor_scalar_mul(
                                dst[:, oc * 512:(oc + 1) * 512], tscr[:, :], scale))
                        p.ms("v")
                    p.wait("t", p.ms("v"))

                wt_toks = {}
                for tt in range(NT4):
                    t0 = tt * 512
                    p.wait("g", ("t", p.cnt["t"]))
                    p.waits("q", [("t", p.cnt["t"])])
                    load_wT(w2T, wd["L2_attn_proj_qk_w"], stage3)
                    p.waits("t", p.dma_tokens("q"))
                    for oc in range(NCH):
                        p.wait("t", ("v", p.cnt["v"]))
                        for dc in range(NCH):
                            p.op("t", lambda t, oc=oc, dc=dc, t0=t0: t.matmul(
                                pb[4][:, :],
                                lhsT=w2T[:, dc * D + oc * 128: dc * D + (oc + 1) * 128],
                                rhs=xn[:, dc * T + t0: dc * T + t0 + 512],
                                start=(dc == 0), stop=(dc == NCH - 1)))
                        pt_ = p.ms("t")
                        p.wait("v", pt_)
                        p.op("v", lambda v, oc=oc: v.tensor_scalar(
                            out=qk_t[:, oc * 512:(oc + 1) * 512], in0=pb[4][:, :],
                            scalar1=bqkc[:, oc:oc + 1], scalar2=None, op0=OP.add))
                        p.ms("v")
                    p.wait("t", p.ms("v"))
                    p.wait("g", ("t", p.cnt["t"]))
                    p.waits("q", [("t", p.cnt["t"])])
                    load_wT(w2T, wd["L2_attn_proj_v_w"], stage3)
                    p.waits("t", p.dma_tokens("q"))
                    for oc in range(NCH):
                        p.wait("t", ("v", p.cnt["v"]))
                        for dc in range(NCH):
                            p.op("t", lambda t, oc=oc, dc=dc, t0=t0: t.matmul(
                                pb[4][:, :],
                                lhsT=w2T[:, dc * D + oc * 128: dc * D + (oc + 1) * 128],
                                rhs=xn[:, dc * T + t0: dc * T + t0 + 512],
                                start=(dc == 0), stop=(dc == NCH - 1)))
                        pt_ = p.ms("t")
                        p.wait("v", pt_)
                        p.op("v", lambda v, oc=oc: v.tensor_scalar(
                            out=vi_t[:, oc * 512:(oc + 1) * 512], in0=pb[4][:, :],
                            scalar1=bvic[:, oc:oc + 1], scalar2=None, op0=OP.add))
                        p.ms("v")
                    p.wait("t", p.ms("v"))
                    p.wait("g", ("t", p.cnt["t"]))
                    p.waits("q", [("t", p.cnt["t"])])
                    load_wT(w2T, wd["L2_attn_in_proj_w"][0:D], stage3)
                    p.waits("t", p.dma_tokens("q"))
                    proj_tile2(q2_t, qk_t, biqc, scale=0.08838834764831845)
                    p.wait("g", ("t", p.cnt["t"]))
                    p.waits("q", [("t", p.cnt["t"])])
                    load_wT(w2T, wd["L2_attn_in_proj_w"][D:2 * D], stage3)
                    p.waits("t", p.dma_tokens("q"))
                    proj_tile2(k2_t, qk_t, bikc)
                    p.wait("g", ("t", p.cnt["t"]))
                    p.waits("q", [("t", p.cnt["t"])])
                    load_wT(w2T, wd["L2_attn_in_proj_w"][2 * D:3 * D], stage3)
                    p.waits("t", p.dma_tokens("q"))
                    proj_tile2(v2_t, vi_t, bivc)
                    for h in range(NCH):
                        qh = q2_t[:, h * 512:(h + 1) * 512].rearrange("p (s f) -> p s f", f=4)
                        kh = k2_t[:, h * 512:(h + 1) * 512].rearrange("p (s f) -> p s f", f=4)
                        vh = v2_t[:, h * 512:(h + 1) * 512].rearrange("p (s f) -> p s f", f=4)
                        for i in range(4):
                            js = [j for j in range(4) if j != i]
                            for jj, j in enumerate(js):
                                p.wait("v", ("t", p.cnt["t"]))
                                p.op("v", lambda v, qh=qh, kh=kh, i=i, j=j: v.tensor_mul(
                                    tscr[:, 0:128], qh[:, :, i], kh[:, :, j]))
                                p.selfsync("v")
                                p.op("v", lambda v: v.tensor_copy(
                                    out=pbf[:, 0:128], in_=tscr[:, 0:128]))
                                p.wait("t", p.ms("v"))
                                p.op("t", lambda t, jj=jj: t.matmul(
                                    pb[5][0:1, jj * 128:(jj + 1) * 128],
                                    lhsT=onesb[:, 0:1], rhs=pbf[:, 0:128],
                                    start=True, stop=True, skip_group_check=True))
                                p.ms("t")
                            srt = p.ms("t")
                            p.wait("s", srt)
                            p.op("s", lambda s: s.copy(out=rL[0:1, 0:384], in_=pb[5][0:1, 0:384]))
                            p.wait("v", p.ms("s"))
                            p.op("v", lambda v: v.tensor_tensor(
                                out=rs(0, 128), in0=rL[0:1, 0:128],
                                in1=rL[0:1, 128:256], op=OP.max))
                            p.selfsync("v")
                            p.op("v", lambda v: v.tensor_tensor(
                                out=rs(0, 128), in0=rs(0, 128),
                                in1=rL[0:1, 256:384], op=OP.max))
                            p.selfsync("v")
                            for jj in range(3):
                                p.op("v", lambda v, jj=jj: v.tensor_sub(
                                    rS[0:1, 512 + jj * 128: 512 + (jj + 1) * 128],
                                    rL[0:1, jj * 128:(jj + 1) * 128], rs(0, 128)))
                            p.wait("s", p.ms("v"))
                            p.wait("s", p.ms("v"))
                            p.op("s", lambda s: s.activation(
                                out=rS[0:1, 512:896], in_=rS[0:1, 512:896], func=AF.Exp))
                            p.wait("v", p.ms("s"))
                            p.op("v", lambda v: v.tensor_add(
                                rs(0, 128), rS[0:1, 512:640], rS[0:1, 640:768]))
                            p.selfsync("v")
                            p.op("v", lambda v: v.tensor_add(
                                rs(0, 128), rs(0, 128), rS[0:1, 768:896]))
                            p.selfsync("v")
                            p.op("v", lambda v: v.reciprocal(rs(0, 128), rs(0, 128)))
                            p.selfsync("v")
                            for jj in range(3):
                                p.op("v", lambda v, jj=jj: v.tensor_mul(
                                    rS[0:1, 512 + jj * 128: 512 + (jj + 1) * 128],
                                    rS[0:1, 512 + jj * 128: 512 + (jj + 1) * 128],
                                    rs(0, 128)))
                            p.selfsync("v")
                            p.wait("t", p.ms("v"))
                            for jj, j in enumerate(js):
                                bt = bcast_row(rS[0:1, 512 + jj * 128: 512 + (jj + 1) * 128],
                                               128, 128, pb[6], accum=False)
                                p.wait("v", bt)
                                if jj == 0:
                                    p.op("v", lambda v, vh=vh, j=j: v.tensor_mul(
                                        tscr[:, 128:256], vh[:, :, j], pb[6][:, 0:128]))
                                else:
                                    p.op("v", lambda v, vh=vh, j=j: v.tensor_mul(
                                        tscr[:, 0:128], vh[:, :, j], pb[6][:, 0:128]))
                                    p.op("v", lambda v: v.tensor_add(
                                        tscr[:, 128:256], tscr[:, 128:256], tscr[:, 0:128]))
                                p.wait("t", p.ms("v"))
                            p.op("v", lambda v, h=h, i=i: v.tensor_copy(
                                out=o2_t[:, h * 512:(h + 1) * 512].rearrange(
                                    "p (s f) -> p s f", f=4)[:, :, i], in_=tscr[:, 128:256]))
                            p.ms("v")
                    p.wait("t", p.ms("v"))
                    p.wait("g", ("t", p.cnt["t"]))
                    p.waits("q", [("t", p.cnt["t"])])
                    load_wT(w2T, wd["L2_attn_out_proj_w"], stage3)
                    p.waits("t", p.dma_tokens("q"))
                    for oc in range(NCH):
                        p.wait("t", ("v", p.cnt["v"]))
                        for dc in range(NCH):
                            p.op("t", lambda t, oc=oc, dc=dc: t.matmul(
                                pb[4][:, :],
                                lhsT=w2T[:, dc * D + oc * 128: dc * D + (oc + 1) * 128],
                                rhs=o2_t[:, dc * 512:(dc + 1) * 512],
                                start=(dc == 0), stop=(dc == NCH - 1)))
                        pt_ = p.ms("t")
                        p.wait("v", pt_)
                        p.op("v", lambda v, oc=oc: v.tensor_scalar(
                            out=tscr[:, :], in0=pb[4][:, :],
                            scalar1=boc2[:, oc:oc + 1], scalar2=None, op0=OP.add))
                        p.selfsync("v")
                        p.op("v", lambda v, oc=oc, t0=t0: v.tensor_add(
                            x[:, oc * T + t0: oc * T + t0 + 512],
                            x[:, oc * T + t0: oc * T + t0 + 512], tscr[:, :]))
                        p.ms("v")
                    p.wait("t", p.ms("v"))
                p.barrier()
                a2.close()
                l2.close()
                p.barrier()
                if "l2x" in taps:
                    tap_outs["l2x"] = nc.dram_tensor("tap_l2x", [128, T], F32,
                                                     kind="ExternalOutput").ap()
                    p.dma("q", tap_outs["l2x"], x[:, 0:T])
                if phase == "l2":
                    raise StopPhase

            if nlayers >= 4:
                p.barrier()
                l3 = ExitStack()
                sb3 = lambda n, s, d=F32: l3.enter_context(nc.sbuf_tensor(n, s, d))
                sqs3 = [sb3(f"l3sq{i}", [128, 512], BF16) for i in range(4)]
                ln_stats(lambda c: x[:, c * T:(c + 1) * T], rA, rB, sqs3)
                ln_apply(lambda c: x[:, c * T:(c + 1) * T], rA, rB,
                         lambda c: xn[:, c * T:(c + 1) * T])
                p.barrier()

                def load_wT3(dst, wsrc, stage, ncol=NCH):
                    for tkq in p.dma_tokens("q"):
                        p.wait("g", tkq)
                    toks = {}
                    for ot in range(wsrc.shape[0] // 128):
                        buf = stage[ot % 2]
                        if ot >= 2:
                            p.wait("g", toks[ot - 2])
                        ldt = p.dma("g", buf[:, 0:wsrc.shape[1]], wsrc[ot * 128:(ot + 1) * 128, :])
                        p.wait("q", ldt)
                        toks[ot] = p.dma_tr(
                            "q", dst[:].rearrange("p (c o) -> p c o", c=ncol)[:, :, ot * 128:(ot + 1) * 128],
                            buf[:, 0:wsrc.shape[1]].rearrange("o (c e) -> o c e", c=ncol))
                    return p.dma_tokens("q")

                ra = ExitStack()
                sbr3 = lambda n, s, d=F32: ra.enter_context(nc.sbuf_tensor(n, s, d))
                wqT3 = sbr3("wqT3", [128, NCH * 512], BF16)
                wkT3 = sbr3("wkT3", [128, NCH * 512], BF16)
                whd = sbr3("whd", [128, NCH * 128], BF16)
                stg3 = [sbr3(f"l3st{i}", [128, D], BF16) for i in range(2)]
                q3 = sbr3("q3", [128, 4 * T], BF16)
                k3 = sbr3("k3", [128, 4 * T], BF16)
                vtm3 = sbr3("vtm3", [128, NTT * 128], BF16)
                dk1 = sbr3("dk1", [128, 512], BF16)
                rcs = sbr3("rcs", [128, 512], BF16)
                rsn = sbr3("rsn", [128, 512], BF16)
                rsc3 = sbr3("rsc3", [128, 512])
                ost = sbr3("ost", [128, 512], BF16)
                osq = sbr3("osq", [128, 512], BF16)
                gnwc = sbr3("gnwc", [128, 8]); gnbc = sbr3("gnbc", [128, 8])
                sgs = sbr3("sgs", [128, 512])
                p.barrier()
                tkq_ = load_wT3(wqT3, wd["L3_attn_Wq"], stg3)
                tkk_ = load_wT3(wkT3, wd["L3_attn_Wk"], stg3)
                g1_ = p.dma("g", gnwc[:], wd["L3_attn_gn_weight"].rearrange("(c p) -> p c", p=128), slow=True, ch=0)
                g2_ = p.dma("g", gnbc[:], wd["L3_attn_gn_bias"].rearrange("(c p) -> p c", p=128), slow=True, ch=1)
                p.waits("v", [g1_, g2_])
                p.waits("t", tkq_ + tkk_)
                for (dst, wsrc, gtab, qscale) in ((q3, wqT3, "gqt", True), (k3, wkT3, "gkt", False)):
                    for qc in range(4):
                        for tt in range(NT4):
                            p.wait("t", ("v", p.cnt["v"]))
                            for dc in range(NCH):
                                p.op("t", lambda t, qc=qc, tt=tt, dc=dc, wsrc=wsrc: t.matmul(
                                    pb[4][:, :],
                                    lhsT=wsrc[:, dc * 512 + qc * 128: dc * 512 + (qc + 1) * 128],
                                    rhs=xn[:, dc * T + tt * 512: dc * T + (tt + 1) * 512],
                                    start=(dc == 0), stop=(dc == NCH - 1)))
                            mt = p.ms("t")
                            ldc = p.dma("g", dk1[:], cd[gtab][:, qc * T + tt * 512: qc * T + (tt + 1) * 512], ch=0)
                            ldc2 = p.dma("g", rcs[:], cd["rcos"][:, tt * 512:(tt + 1) * 512], ch=1)
                            ldc3 = p.dma("g", rsn[:], cd["rsin"][:, tt * 512:(tt + 1) * 512], ch=2)
                            p.waits("v", [mt, ldc, ldc2, ldc3])
                            for bs in (0, 64):
                                p.op("v", lambda v, bs=bs: v.tensor_scalar_mul(
                                    rsc3[bs:bs + 32, :], pb[4][bs + 32:bs + 64, :], -1.0))
                                p.op("v", lambda v, bs=bs: v.tensor_copy(
                                    out=rsc3[bs + 32:bs + 64, :], in_=pb[4][bs:bs + 32, :]))
                            p.selfsync("v")
                            p.op("v", lambda v: v.tensor_mul(tscr[:, :], rsc3[:], rsn[:]))
                            p.op("v", lambda v: v.tensor_mul(rsc3[:], pb[4][:, :], rcs[:]))
                            p.selfsync("v")
                            p.op("v", lambda v: v.tensor_add(tscr[:, :], tscr[:, :], rsc3[:]))
                            p.selfsync("v")
                            if qscale:
                                p.op("v", lambda v: v.tensor_scalar_mul(tscr[:, :], tscr[:, :], 0.125))
                                p.selfsync("v")
                            p.op("v", lambda v, dst=dst, qc=qc, tt=tt: v.tensor_mul(
                                dst[:, qc * T + tt * 512: qc * T + (tt + 1) * 512],
                                tscr[:, :], dk1[:]))
                            p.ms("v")
                            p.wait("g", p.ms("v"))
                p.barrier()
                for h in range(8):
                    ch3, hb3 = h // 2, 64 * (h % 2)
                    p.wait("g", ("t", p.cnt["t"]))
                    tkv_ = load_wT3(whd, wd["L3_attn_Wv"][h * 128:(h + 1) * 128, :], stg3)
                    p.waits("t", tkv_)
                    for ti in range(NTT):
                        p.wait("t", ("v", p.cnt["v"]))
                        for dc in range(NCH):
                            p.op("t", lambda t, ti=ti, dc=dc: t.matmul(
                                pb[5][:, 0:128],
                                lhsT=xn[:, dc * T + ti * 128: dc * T + (ti + 1) * 128],
                                rhs=whd[:, dc * 128:(dc + 1) * 128],
                                start=(dc == 0), stop=(dc == NCH - 1)))
                        vt3 = p.ms("t")
                        p.wait("v", vt3)
                        p.op("v", lambda v, ti=ti: v.tensor_copy(
                            out=vtm3[:, ti * 128:(ti + 1) * 128], in_=pb[5][:, 0:128]))
                        p.ms("v")
                    p.wait("v", ("t", p.cnt["t"]))
                    p.wait("g", ("t", p.cnt["t"]))
                    tkg_ = load_wT3(whd, wd["L3_attn_Wg"][h * 128:(h + 1) * 128, :], stg3)
                    p.waits("t", tkg_)
                    for qt in range(NT4):
                        for j in range(4 * qt + 4):
                            scb = pb[j % 2]
                            p.op("t", lambda t, j=j, qt=qt, ch3=ch3, hb3=hb3, scb=scb: t.matmul(
                                scb[:, :],
                                lhsT=k3[hb3:hb3 + 64, ch3 * T + j * 128: ch3 * T + (j + 1) * 128],
                                rhs=q3[hb3:hb3 + 64, ch3 * T + qt * 512: ch3 * T + (qt + 1) * 512],
                                start=True, stop=True))
                            mt = p.ms("t")
                            eb = sqs3[j % 2]
                            p.wait("v", mt)
                            if j >= 2:
                                p.wait("v", ("t", p.cnt["t"]))
                            if j >= 4 * qt:
                                jj = j - 4 * qt
                                p.op("v", lambda v, eb=eb, scb=scb, jj=jj: v.tensor_mul(
                                    eb[:, :], scb[:, :], mext[:, 384 - 128 * jj: 896 - 128 * jj]))
                            else:
                                p.op("v", lambda v, eb=eb, scb=scb: v.tensor_copy(
                                    out=eb[:, :], in_=scb[:, :]))
                            st3 = p.ms("v")
                            p.wait("t", st3)
                            p.op("t", lambda t, j=j, qt=qt, eb=eb: t.matmul(
                                pb[2][:, :], lhsT=vtm3[:, j * 128:(j + 1) * 128],
                                rhs=eb[:, :], start=(j == 0), stop=(j == 4 * qt + 3),
                                skip_group_check=True))
                            p.ms("t")
                        ott = p.ms("t")
                        p.wait("v", ott)
                        p.op("v", lambda v: v.tensor_copy(out=ost[:], in_=pb[2][:, :]))
                        p.selfsync("v")
                        p.op("v", lambda v: v.tensor_mul(osq[:], ost[:], ost[:]))
                        p.wait("t", p.ms("v"))
                        p.op("t", lambda t: t.matmul(pb[6][0:1, :], lhsT=onesb[:, 0:1],
                                                     rhs=ost[:], start=True, stop=True))
                        p.op("t", lambda t: t.matmul(pb[7][0:1, :], lhsT=onesb[:, 0:1],
                                                     rhs=osq[:], start=True, stop=True))
                        stt = p.ms("t")
                        p.wait("v", stt)
                        p.op("v", lambda v: v.tensor_scalar_mul(rs(0), pb[6][0:1, :], 1.0 / 128.0))
                        p.op("v", lambda v: v.tensor_scalar_mul(rs(1), pb[7][0:1, :], 1.0 / 128.0))
                        p.op("v", lambda v: v.tensor_mul(rs(2), rs(0), rs(0)))
                        p.op("v", lambda v: v.tensor_sub(rs(1), rs(1), rs(2)))
                        p.op("v", lambda v: v.tensor_scalar_add(rs(1), rs(1), LN_EPS))
                        p.wait("s", p.ms("v"))
                        p.op("s", lambda s: s.activation(out=rs(1), in_=rs(1), func=AF.Sqrt))
                        p.wait("v", p.ms("s"))
                        p.op("v", lambda v: v.reciprocal(rs(1), rs(1)))
                        p.selfsync("v")
                        p.wait("t", p.ms("v"))
                        b1t = bcast_row(rs(0), 128, 512, pb[4])
                        b2t = bcast_row(rs(1), 128, 512, pb[5])
                        p.wait("s", b1t)
                        p.op("s", lambda s: s.copy(out=rsc3[:], in_=pb[4][:, :]))
                        p.waits("v", [b2t, p.ms("s")])
                        p.op("v", lambda v: v.tensor_sub(tscr[:, :], pb[2][:, :], rsc3[:]))
                        p.selfsync("v")
                        p.op("v", lambda v: v.tensor_mul(tscr[:, :], tscr[:, :], pb[5][:, :]))
                        p.selfsync("v")
                        p.wait("s", p.ms("v"))
                        p.op("v", lambda v, h=h: v.tensor_scalar(
                            out=sgs[:], in0=tscr[:, :], scalar1=gnwc[:, h:h + 1],
                            scalar2=gnbc[:, h:h + 1], op0=OP.mult, op1=OP.add))
                        p.selfsync("v")
                        p.wait("t", p.ms("v"))
                        for dc in range(NCH):
                            p.op("t", lambda t, qt=qt, dc=dc: t.matmul(
                                pb[3][:, :],
                                lhsT=whd[:, dc * 128:(dc + 1) * 128],
                                rhs=xn[:, dc * T + qt * 512: dc * T + (qt + 1) * 512],
                                start=(dc == 0), stop=(dc == NCH - 1)))
                        gtt = p.ms("t")
                        p.wait("s", gtt)
                        p.op("s", lambda s: s.activation(out=rsc3[:], in_=pb[3][:, :],
                                                         func=AF.Sigmoid))
                        p.wait("v", p.ms("s"))
                        p.op("v", lambda v: v.tensor_mul(rsc3[:], pb[3][:, :], rsc3[:]))
                        p.selfsync("v")
                        p.op("v", lambda v: v.tensor_mul(ost[:], sgs[:], rsc3[:]))
                        p.wait("t", p.ms("v"))
                        p.wait("g", p.ms("v"))
                        tko_ = load_wT3(whd, wd["L3_attn_Wo"][:, h * 128:(h + 1) * 128], stg3, ncol=1)
                        p.waits("t", tko_)
                        for oc in range(NCH):
                            p.wait("t", ("v", p.cnt["v"]))
                            p.op("t", lambda t, oc=oc: t.matmul(
                                pb[4][:, :], lhsT=whd[:, oc * 128:(oc + 1) * 128],
                                rhs=ost[:], start=True, stop=True))
                            o3t = p.ms("t")
                            p.wait("v", o3t)
                            p.op("v", lambda v, oc=oc, qt=qt: v.tensor_add(
                                x[:, oc * T + qt * 512: oc * T + (qt + 1) * 512],
                                x[:, oc * T + qt * 512: oc * T + (qt + 1) * 512],
                                pb[4][:, :]))
                            p.ms("v")
                        p.wait("t", p.ms("v"))
                        p.wait("g", ("v", p.cnt["v"]))
                        tkg_ = load_wT3(whd, wd["L3_attn_Wg"][h * 128:(h + 1) * 128, :], stg3)
                        p.waits("t", tkg_)
                p.barrier()
                ra.close()
                if "l3att" in taps:
                    tap_outs["l3att"] = nc.dram_tensor("tap_l3att", [128, T], F32,
                                                       kind="ExternalOutput").ap()
                    p.dma("q", tap_outs["l3att"], x[:, 0:T])
                if phase == "l3att":
                    raise StopPhase

                xs = xn

                def sub_layer(sl_):
                    sa3 = ExitStack()
                    sbs = lambda n, s, d=F32: sa3.enter_context(nc.sbuf_tensor(n, s, d))
                    wkv_sb = sbs(f"wkv{sl_}", [128, NCH * 1024], BF16)
                    wq_sb = sbs(f"wq{sl_}", [128, NCH * 512], BF16)
                    wo_sb = sbs(f"wo{sl_}", [128, 4 * 1024], BF16)
                    k3s = sbs(f"k3s{sl_}", [128, 4 * 1024], BF16)
                    vtms = sbs(f"vtms{sl_}", [128, 8 * 520], BF16)
                    scr3 = sbs(f"scr3{sl_}", [128, 512])
                    h_t = sbs(f"ht{sl_}", [128, NCH * 512], BF16)
                    qqi = sbs(f"qqi{sl_}", [128, 4 * 128], BF16)
                    sqb = sbs(f"sqb{sl_}", [128, 512], BF16)
                    rw2 = sbs(f"rw2{sl_}", [2, 512], BF16)
                    qsc = sbs(f"qsc{sl_}", [128, 2])
                    xcs = sbs(f"xcs{sl_}", [128, 512], BF16)
                    xsn = sbs(f"xsn{sl_}", [128, 512], BF16)
                    ind2c = sbs(f"i2c{sl_}", [128, 2], BF16)
                    ind2r = sbs(f"i2r{sl_}", [2, 128], BF16)
                    mrev = sbs(f"mrv{sl_}", [128, 128], BF16)
                    expb = [sbs(f"exb{sl_}{i}", [128, 128], BF16) for i in range(2)]
                    o3b = sbs(f"o3b{sl_}", [128, 128], BF16)
                    p.barrier()
                    lw1 = p.dma("g", wkv_sb[:].rearrange("p (dc o) -> p dc o", dc=NCH),
                                wd["L3_ffn_wqkv"][sl_][:, 512:1536].rearrange(
                                    "(dc p) o -> p dc o", p=128), ch=0)
                    lw2 = p.dma("g", wq_sb[:].rearrange("p (dc o) -> p dc o", dc=NCH),
                                wd["L3_ffn_wqkv"][sl_][:, 0:512].rearrange(
                                    "(dc p) o -> p dc o", p=128), ch=1)
                    lw3 = p.dma("g", wo_sb[:].rearrange("p (c o) -> p c o", c=4),
                                wd["L3_ffn_wo"][sl_].rearrange("(c p) o -> p c o", p=128), ch=2)
                    lq1 = p.dma("g", qsc[0:64, 0:1], wd["L3_ffn_q_scale"][sl_].rearrange("(o d) -> d o", o=1), ch=3, slow=True)
                    lq2 = p.dma("g", qsc[64:128, 0:1], wd["L3_ffn_q_scale"][sl_].rearrange("(o d) -> d o", o=1), ch=3, slow=True)
                    lq3 = p.dma("g", qsc[0:64, 1:2], wd["L3_ffn_k_scale"][sl_].rearrange("(o d) -> d o", o=1), ch=3, slow=True)
                    lq4 = p.dma("g", qsc[64:128, 1:2], wd["L3_ffn_k_scale"][sl_].rearrange("(o d) -> d o", o=1), ch=3, slow=True)
                    lc1 = p.dma("q", ind2c[:], cd["ind2c"])
                    lc2 = p.dma("q", ind2r[:], cd["ind2r"])
                    lc3 = p.dma("q", mrev[:], cd["mrev"])
                    p.waits("t", [lw1, lw2, lw3, lc1, lc2, lc3])
                    p.waits("v", [lq1, lq2, lq3, lq4, lc1, lc2, lc3])
                    p.op("v", lambda v: v.memset(vtms[:], 1.0))
                    p.wait("t", p.ms("v"))

                    def build_ht(src, t0, hbuf, sbuf_):
                        p.wait("v", ("t", p.cnt["t"]))
                        for c in range(NCH):
                            if c < 4:
                                p.op("v", lambda v, c=c, t0=t0, src=src, hbuf=hbuf: v.tensor_copy(
                                    out=hbuf[:, c * 512:(c + 1) * 512],
                                    in_=src[:, c * T + t0: c * T + t0 + 512]))
                            elif t0 == 0:
                                p.op("v", lambda v, sbuf_=sbuf_: v.memset(sbuf_[:, 0:1], 0.0))
                                p.op("v", lambda v, c=c, src=src, sbuf_=sbuf_: v.tensor_copy(
                                    out=sbuf_[:, 1:512], in_=src[:, c * T: c * T + 511]))
                                p.selfsync("v")
                                p.op("v", lambda v, c=c, hbuf=hbuf, sbuf_=sbuf_: v.tensor_copy(
                                    out=hbuf[:, c * 512:(c + 1) * 512], in_=sbuf_[:, :]))
                            else:
                                p.op("v", lambda v, c=c, t0=t0, src=src, hbuf=hbuf: v.tensor_copy(
                                    out=hbuf[:, c * 512:(c + 1) * 512],
                                    in_=src[:, c * T + t0 - 1: c * T + t0 + 511]))
                        p.selfsync("v")
                        p.wait("t", p.ms("v"))

                    def norm_xpos(psrc, dst_ap, col, ctab, stab, width):
                        p.wait("s", ("t", p.cnt["t"]))
                        p.op("s", lambda s: s.activation(out=sqb[:, 0:width], in_=psrc,
                                                         func=AF.Square))
                        p.wait("t", p.ms("s"))
                        p.op("t", lambda t: t.matmul(pb[6][0:2, 0:width], lhsT=ind2c[:],
                                                     rhs=sqb[:, 0:width], start=True, stop=True))
                        nt_ = p.ms("t")
                        p.wait("s", nt_)
                        p.op("s", lambda s: s.activation(out=scr3[0:2, 0:width],
                                                         in_=pb[6][0:2, 0:width], func=AF.Sqrt))
                        p.wait("v", p.ms("s"))
                        p.op("v", lambda v: v.reciprocal(scr3[0:2, 0:width], scr3[0:2, 0:width]))
                        p.selfsync("v")
                        p.op("v", lambda v: v.tensor_copy(out=rw2[0:2, 0:width], in_=scr3[0:2, 0:width]))
                        p.selfsync("v")
                        p.wait("t", p.ms("v"))
                        p.op("t", lambda t: t.matmul(pb[7][0:128, 0:width], lhsT=ind2r[:],
                                                     rhs=rw2[0:2, 0:width], start=True, stop=True))
                        bt_ = p.ms("t")
                        p.wait("s", bt_)
                        p.op("s", lambda s: s.copy(out=tscr[:, 0:width], in_=pb[7][0:128, 0:width]))
                        p.wait("v", p.ms("s"))
                        p.op("v", lambda v: v.tensor_mul(scr3[:, 0:width], psrc, tscr[:, 0:width]))
                        p.selfsync("v")
                        p.op("v", lambda v, col=col: v.tensor_scalar(
                            out=scr3[:, 0:width], in0=scr3[:, 0:width],
                            scalar1=qsc[:, col:col + 1], scalar2=None, op0=OP.mult))
                        p.selfsync("v")
                        for bs in (0, 64):
                            p.op("v", lambda v, bs=bs: v.tensor_scalar_mul(
                                tscr[bs:bs + 32, 0:width], scr3[bs + 32:bs + 64, 0:width], -1.0))
                            p.op("v", lambda v, bs=bs: v.tensor_copy(
                                out=tscr[bs + 32:bs + 64, 0:width], in_=scr3[bs:bs + 32, 0:width]))
                        p.selfsync("v")
                        p.op("v", lambda v: v.tensor_mul(tscr[:, 0:width], tscr[:, 0:width], stab))
                        p.selfsync("v")
                        p.op("v", lambda v: v.tensor_mul(scr3[:, 0:width], scr3[:, 0:width], ctab))
                        p.selfsync("v")
                        p.op("v", lambda v, dst_ap=dst_ap: v.tensor_add(
                            dst_ap, tscr[:, 0:width], scr3[:, 0:width]))
                        p.ms("v")
                        p.wait("s", p.ms("v"))

                    for tt in range(NT4):
                        t0 = tt * 512
                        build_ht(xs, t0, h_t, sqb)
                        ltc = p.dma("g", xcs[:], cd["xkc"][:, t0:t0 + 512], ch=0)
                        lts = p.dma("g", xsn[:] if False else xsn[:, 0:512], cd["xks"][:, t0:t0 + 512], ch=1)
                        p.waits("v", [ltc, lts])
                        for kc in range(4):
                            p.wait("t", ("v", p.cnt["v"]))
                            for dc in range(NCH):
                                p.op("t", lambda t, kc=kc, dc=dc: t.matmul(
                                    pb[4][:, :],
                                    lhsT=wkv_sb[:, dc * 1024 + kc * 128: dc * 1024 + (kc + 1) * 128],
                                    rhs=h_t[:, dc * 512:(dc + 1) * 512],
                                    start=(dc == 0), stop=(dc == NCH - 1)))
                            kt3 = p.ms("t")
                            p.wait("v", kt3)
                            norm_xpos(pb[4][:, :],
                                      k3s[:, kc * 1024 + ((4 * tt) % 8) * 128:
                                          kc * 1024 + ((4 * tt) % 8) * 128 + 512], 1,
                                      xcs[:, 0:512], xsn[:, 0:512], 512)
                        p.wait("t", p.ms("v"))
                        for til in range(4):
                            ti = tt * 4 + til
                            for hp2 in range(2):
                                p.wait("t", ("v", p.cnt["v"]))
                                for dc in range(NCH):
                                    p.op("t", lambda t, til=til, hp2=hp2, dc=dc: t.matmul(
                                        pb[5][:, 0:256],
                                        lhsT=h_t[:, dc * 512 + til * 128: dc * 512 + (til + 1) * 128],
                                        rhs=wkv_sb[:, dc * 1024 + 512 + hp2 * 256: dc * 1024 + 512 + (hp2 + 1) * 256],
                                        start=(dc == 0), stop=(dc == NCH - 1)))
                                vt4 = p.ms("t")
                                p.wait("v", vt4)
                                for sub in range(4):
                                    slot = hp2 * 4 + sub
                                    p.op("v", lambda v, ti=ti, slot=slot, sub=sub: v.tensor_copy(
                                        out=vtms[:, (ti % 8) * 520 + slot * 65: (ti % 8) * 520 + slot * 65 + 64],
                                        in_=pb[5][:, sub * 64:(sub + 1) * 64]))
                                p.ms("v")
                        p.wait("t", p.ms("v"))
                        ltc2 = p.dma("g", xcs[:], cd["xqc"][:, t0:t0 + 512], ch=0)
                        lts2 = p.dma("g", xsn[:, 0:512], cd["xqs"][:, t0:t0 + 512], ch=1)
                        p.waits("v", [ltc2, lts2])
                        for qig in range(4):
                            qi = tt * 4 + qig
                            for qc in range(4):
                                p.wait("t", ("v", p.cnt["v"]))
                                for dc in range(NCH):
                                    p.op("t", lambda t, qc=qc, dc=dc, qig=qig: t.matmul(
                                        pb[4][:, 0:128],
                                        lhsT=wq_sb[:, dc * 512 + qc * 128: dc * 512 + (qc + 1) * 128],
                                        rhs=h_t[:, dc * 512 + qig * 128: dc * 512 + (qig + 1) * 128],
                                        start=(dc == 0), stop=(dc == NCH - 1)))
                                qt3 = p.ms("t")
                                p.wait("v", qt3)
                                norm_xpos(pb[4][:, 0:128], qqi[:, qc * 128:(qc + 1) * 128], 0,
                                          xcs[:, qig * 128:(qig + 1) * 128],
                                          xsn[:, qig * 128:(qig + 1) * 128], 128)
                                p.op("v", lambda v, qc=qc: v.tensor_scalar_mul(
                                    qqi[:, qc * 128:(qc + 1) * 128],
                                    qqi[:, qc * 128:(qc + 1) * 128], 8.0))
                                p.ms("v")
                            p.wait("t", p.ms("v"))
                            kjs = [j for j in range(max(0, qi - 4), qi + 1)]
                            for h3 in range(8):
                                ch4, hb4 = h3 // 2, 64 * (h3 % 2)
                                for ji, kj in enumerate(kjs):
                                    scb = pb[ji % 2]
                                    p.op("t", lambda t, kj=kj, ch4=ch4, hb4=hb4, scb=scb: t.matmul(
                                        scb[:, 0:128],
                                        lhsT=k3s[hb4:hb4 + 64, ch4 * 1024 + (kj % 8) * 128: ch4 * 1024 + (kj % 8) * 128 + 128],
                                        rhs=qqi[hb4:hb4 + 64, ch4 * 128:(ch4 + 1) * 128],
                                        start=True, stop=True))
                                    smt = p.ms("t")
                                    eb3 = expb[ji % 2]
                                    p.wait("s", smt)
                                    if ji >= 2:
                                        p.wait("s", ("t", p.cnt["t"]))
                                    p.op("s", lambda s, scb=scb: s.activation(
                                        out=scr3[:, 0:128], in_=scb[:, 0:128],
                                        func=AF.Exp, bias=cp("shift8")))
                                    est = p.ms("s")
                                    p.wait("v", est)
                                    if kj == qi:
                                        p.op("v", lambda v, eb3=eb3: v.tensor_mul(
                                            eb3[:], scr3[:, 0:128], mext[:, 384:512]))
                                    elif kj == qi - 4:
                                        p.op("v", lambda v, eb3=eb3: v.tensor_mul(
                                            eb3[:], scr3[:, 0:128], mrev[:]))
                                    else:
                                        p.op("v", lambda v, eb3=eb3: v.tensor_copy(
                                            out=eb3[:], in_=scr3[:, 0:128]))
                                    ev3 = p.ms("v")
                                    p.wait("t", ev3)
                                    p.op("t", lambda t, kj=kj, h3=h3, eb3=eb3, ji=ji, last=(ji == len(kjs) - 1): t.matmul(
                                        pb[2][0:65, 0:128],
                                        lhsT=vtms[:, (kj % 8) * 520 + h3 * 65: (kj % 8) * 520 + h3 * 65 + 65],
                                        rhs=eb3[:], start=(ji == 0), stop=last,
                                        skip_group_check=True))
                                    p.ms("t")
                                ovt = p.ms("t")
                                p.wait("v", ovt)
                                p.op("v", lambda v: v.tensor_copy(out=rL[0:1, 0:128],
                                                                  in_=pb[2][64:65, 0:128]))
                                p.selfsync("v")
                                p.op("v", lambda v: v.reciprocal(rL[0:1, 0:128], rL[0:1, 0:128]))
                                p.selfsync("v")
                                p.wait("t", p.ms("v"))
                                blt = bcast_row(rL[0:1, 0:128], 64, 128, pb[3])
                                p.wait("s", blt)
                                p.op("s", lambda s: s.copy(out=scr3[0:64, 128:256], in_=pb[3][0:64, 0:128]))
                                p.wait("v", p.ms("s"))
                                p.op("v", lambda v, hb4=hb4: v.tensor_mul(
                                    o3b[hb4:hb4 + 64, :], pb[2][0:64, 0:128], scr3[0:64, 128:256]))
                                p.selfsync("v")
                                p.wait("s", p.ms("v"))
                                p.wait("t", p.ms("v"))
                                for oc in range(NCH):
                                    p.wait("t", ("v", p.cnt["v"]))
                                    p.op("t", lambda t, h3=h3, hb4=hb4, oc=oc: t.matmul(
                                        pb[4][:, 0:128],
                                        lhsT=wo_sb[hb4:hb4 + 64, (h3 // 2) * 1024 + oc * 128: (h3 // 2) * 1024 + (oc + 1) * 128],
                                        rhs=o3b[hb4:hb4 + 64, :], start=True, stop=True))
                                    xot = p.ms("t")
                                    p.wait("v", xot)
                                    p.op("v", lambda v, oc=oc, qi=qi: v.tensor_add(
                                        tscr[:, 0:128], pb[4][:, 0:128],
                                        xs[:, oc * T + qi * 128: oc * T + (qi + 1) * 128]))
                                    p.selfsync("v")
                                    p.op("v", lambda v, oc=oc, qi=qi: v.tensor_copy(
                                        out=xs[:, oc * T + qi * 128: oc * T + (qi + 1) * 128],
                                        in_=tscr[:, 0:128]))
                                    p.ms("v")
                                p.wait("t", p.ms("v"))
                    p.barrier()
                    sa3.close()
                    ff3 = ExitStack()
                    sff = lambda n, s, d=F32: ff3.enter_context(nc.sbuf_tensor(n, s, d))
                    h_tf = sff(f"fht{sl_}", [128, NCH * 512], BF16)
                    hn_t = sff(f"fhn{sl_}", [128, NCH * 512], BF16)
                    h1b = sff(f"fh1{sl_}", [128, 22 * 512], BF16)
                    w1s = sff(f"fw1{sl_}", [128, NCH * 256], BF16)
                    w2s = sff(f"fw2{sl_}", [128, 22 * 128], BF16)
                    gco = sff(f"fgc{sl_}", [128, 8]); bco = sff(f"fbc{sl_}", [128, 8])
                    b2co = sff(f"fb2{sl_}", [128, 8])
                    b1cs = sff(f"fb1{sl_}", [128, 44])
                    sqb2 = sff(f"fsq{sl_}", [128, 512], BF16)
                    scr4 = sff(f"fsc{sl_}", [128, 512])
                    p.barrier()
                    lg3 = p.dma("g", gco[:], wd["L3_ffn_ln_g"][sl_].rearrange("(c p) -> p c", p=128), slow=True, ch=0)
                    lb3 = p.dma("g", bco[:], wd["L3_ffn_ln_b"][sl_].rearrange("(c p) -> p c", p=128), slow=True, ch=1)
                    lb4 = p.dma("g", b2co[:], wd["L3_ffn_b2"][sl_].rearrange("(c p) -> p c", p=128), slow=True, ch=2)
                    lb5 = p.dma("g", b1cs[:, 0:21], wd["L3_ffn_b1"][sl_][0:2688].rearrange("(hc p) -> p hc", p=128), slow=True, ch=3)
                    lb6 = p.dma("g", b1cs[0:42, 21:22], wd["L3_ffn_b1"][sl_][2688:2730].rearrange("(o p) -> p o", o=1), slow=True, ch=0)
                    lb7 = p.dma("g", b1cs[:, 22:43], wd["L3_ffn_b1"][sl_][2730:5418].rearrange("(hc p) -> p hc", p=128), slow=True, ch=1)
                    lb8 = p.dma("g", b1cs[0:42, 43:44], wd["L3_ffn_b1"][sl_][5418:5460].rearrange("(o p) -> p o", o=1), slow=True, ch=2)
                    p.waits("v", [lg3, lb3, lb4, lb6, lb8])
                    p.waits("s", [lb5, lb7, lb6, lb8])
                    HCS = [128] * 21 + [42]
                    for tt in range(NT4):
                        t0 = tt * 512
                        build_ht(xs, t0, h_tf, sqb2)
                        p.wait("t", ("s", p.cnt["s"]))
                        for c in range(NCH):
                            p.op("t", lambda t, c=c: t.matmul(
                                pb[6][0:1, :], lhsT=onesb[:, 0:1],
                                rhs=h_tf[:, c * 512:(c + 1) * 512],
                                start=(c == 0), stop=(c == NCH - 1)))
                        p.wait("s", ("v", p.cnt["v"]))
                        for c in range(NCH):
                            sq_ = sqb2
                            if c > 0:
                                p.wait("s", p.ms("t"))
                            p.op("s", lambda s, c=c: s.activation(
                                out=sq_[:, :], in_=h_tf[:, c * 512:(c + 1) * 512], func=AF.Square))
                            p.wait("t", p.ms("s"))
                            p.op("t", lambda t, c=c: t.matmul(
                                pb[7][0:1, :], lhsT=onesb[:, 0:1], rhs=sq_[:, :],
                                start=(c == 0), stop=(c == NCH - 1)))
                        sst = p.ms("t")
                        p.wait("v", sst)
                        p.op("v", lambda v: v.tensor_scalar_mul(rs(0), pb[6][0:1, :], 1.0 / 1024.0))
                        p.op("v", lambda v: v.tensor_scalar_mul(rs(1), pb[7][0:1, :], 1.0 / 1024.0))
                        p.op("v", lambda v: v.tensor_mul(rs(2), rs(0), rs(0)))
                        p.op("v", lambda v: v.tensor_sub(rs(1), rs(1), rs(2)))
                        p.op("v", lambda v: v.tensor_scalar_add(rs(1), rs(1), LN_EPS))
                        p.wait("s", p.ms("v"))
                        p.op("s", lambda s: s.activation(out=rs(1), in_=rs(1), func=AF.Sqrt))
                        p.wait("v", p.ms("s"))
                        p.op("v", lambda v: v.reciprocal(rs(1), rs(1)))
                        p.selfsync("v")
                        p.wait("t", p.ms("v"))
                        bt1 = bcast_row(rs(0), 128, 512, pb[4])
                        bt2 = bcast_row(rs(1), 128, 512, pb[5])
                        p.waits("v", [bt1, bt2])
                        for c in range(NCH):
                            p.op("v", lambda v, c=c: v.tensor_sub(
                                tscr[:, :], h_tf[:, c * 512:(c + 1) * 512], pb[4][:, 0:512]))
                            p.selfsync("v")
                            p.op("v", lambda v, c=c: v.tensor_mul(
                                tscr[:, :], tscr[:, :], pb[5][:, 0:512]))
                            p.wait("s", p.ms("v"))
                            p.op("s", lambda s, c=c: s.activation(
                                out=scr4[:, :], in_=tscr[:, :],
                                func=AF.Copy, scale=gco[:, c:c + 1]))
                            p.wait("v", p.ms("s"))
                            p.op("v", lambda v, c=c: v.tensor_scalar(
                                out=hn_t[:, c * 512:(c + 1) * 512], in0=scr4[:, :],
                                scalar1=bco[:, c:c + 1], scalar2=None, op0=OP.add))
                            p.selfsync("v")
                        p.wait("t", p.ms("s"))
                        for hc in range(22):
                            kw = HCS[hc]
                            p.wait("g", ("t", p.cnt["t"]))
                            la_ = p.dma("g", w1s[:, 0:NCH * 128].rearrange("p (dc o) -> p dc o", dc=NCH)[:, :, 0:kw],
                                        wd["L3_ffn_w1"][sl_][:, hc * 128: hc * 128 + kw].rearrange(
                                            "(dc p) o -> p dc o", p=128), ch=0)
                            lg_ = p.dma("g", w1s[:, NCH * 128:NCH * 256].rearrange("p (dc o) -> p dc o", dc=NCH)[:, :, 0:kw],
                                        wd["L3_ffn_w1"][sl_][:, 2730 + hc * 128: 2730 + hc * 128 + kw].rearrange(
                                            "(dc p) o -> p dc o", p=128), ch=1)
                            p.waits("t", [la_, lg_])
                            p.wait("t", ("v", p.cnt["v"]))
                            p.wait("t", ("s", p.cnt["s"]))
                            for dc in range(NCH):
                                p.op("t", lambda t, dc=dc, kw=kw: t.matmul(
                                    pb[0][0:kw, :], lhsT=w1s[:, dc * 128: dc * 128 + kw],
                                    rhs=hn_t[:, dc * 512:(dc + 1) * 512],
                                    start=(dc == 0), stop=(dc == NCH - 1)))
                            for dc in range(NCH):
                                p.op("t", lambda t, dc=dc, kw=kw: t.matmul(
                                    pb[1][0:kw, :], lhsT=w1s[:, NCH * 128 + dc * 128: NCH * 128 + dc * 128 + kw],
                                    rhs=hn_t[:, dc * 512:(dc + 1) * 512],
                                    start=(dc == 0), stop=(dc == NCH - 1)))
                            h1t = p.ms("t")
                            p.wait("s", h1t)
                            p.op("s", lambda s, hc=hc, kw=kw: s.activation(
                                out=tscr[0:kw, :], in_=pb[1][0:kw, :], func=AF.Gelu,
                                bias=b1cs[0:kw, 22 + hc: 23 + hc]))
                            p.wait("v", p.ms("s"))
                            p.op("v", lambda v, hc=hc, kw=kw: v.tensor_scalar(
                                out=scr4[0:kw, :], in0=pb[0][0:kw, :],
                                scalar1=b1cs[0:kw, hc:hc + 1], scalar2=None, op0=OP.add))
                            p.selfsync("v")
                            p.op("v", lambda v, hc=hc, kw=kw: v.tensor_mul(
                                h1b[:, hc * 512:(hc + 1) * 512][0:kw, :], scr4[0:kw, :], tscr[0:kw, :]))
                            p.ms("v")
                        p.wait("t", p.ms("v"))
                        for oc in range(NCH):
                            p.wait("g", ("t", p.cnt["t"]))
                            lw2_ = p.dma("g", w2s[:, 0:21 * 128].rearrange("p (hc o) -> p hc o", hc=21),
                                         wd["L3_ffn_w2"][sl_][0:2688, oc * 128:(oc + 1) * 128].rearrange(
                                             "(hc p) o -> p hc o", p=128), ch=0)
                            lw3_ = p.dma("g", w2s[0:42, 21 * 128:22 * 128],
                                         wd["L3_ffn_w2"][sl_][2688:2730, oc * 128:(oc + 1) * 128], ch=1)
                            p.waits("t", [lw2_, lw3_])
                            p.wait("t", ("v", p.cnt["v"]))
                            for hc in range(22):
                                kw = HCS[hc]
                                p.op("t", lambda t, hc=hc, kw=kw: t.matmul(
                                    pb[4][:, :], lhsT=w2s[0:kw, hc * 128:(hc + 1) * 128],
                                    rhs=h1b[:, hc * 512:(hc + 1) * 512][0:kw, :],
                                    start=(hc == 0), stop=(hc == 21)))
                            o4t = p.ms("t")
                            p.wait("v", o4t)
                            p.op("v", lambda v, oc=oc: v.tensor_scalar(
                                out=tscr[:, :], in0=pb[4][:, :],
                                scalar1=b2co[:, oc:oc + 1], scalar2=None, op0=OP.add))
                            p.selfsync("v")
                            p.op("v", lambda v, oc=oc, t0=t0: v.tensor_add(
                                tscr[:, :], tscr[:, :],
                                xs[:, oc * T + t0: oc * T + t0 + 512]))
                            p.selfsync("v")
                            p.op("v", lambda v, oc=oc, t0=t0: v.tensor_copy(
                                out=xs[:, oc * T + t0: oc * T + t0 + 512], in_=tscr[:, :]))
                            p.ms("v")
                        p.wait("t", p.ms("v"))
                    p.barrier()
                    ff3.close()

                for sl_ in range(2):
                    sub_layer(sl_)
                for c in range(NCH):
                    for tt in range(NT4):
                        p.op("v", lambda v, c=c, tt=tt: v.tensor_add(
                            x[:, c * T + tt * 512: c * T + (tt + 1) * 512],
                            x[:, c * T + tt * 512: c * T + (tt + 1) * 512],
                            xs[:, c * T + tt * 512: c * T + (tt + 1) * 512]))
                p.ms("v")
                p.barrier()
                l3.close()
        except StopPhase:
            p.barrier()

        p.barrier()

        if phase != "full":
            p.dma("q", y_d.rearrange("(a bt) d -> a (bt d)", a=128), x[:])
            p.dma_quiesce("q")
            p.dma_quiesce("g")
            p.finalize(block)
            return nc, sorted(tap_outs)
        tmb = [ctx.enter_context(nc.sbuf_tensor(f"smb{i}", [128, D], F32))
               for i in range(2)]
        st_toks = {}
        for i in range(NTT):
            buf = tmb[i % 2]
            if i >= 2:
                p.wait("t", st_toks[i - 2])
            for c in range(NCH):
                bank = pb[c % 2]
                p.op("t", lambda t, i=i, c=c, bank=bank: t.transpose(
                    bank[:, 0:128], in_=x[:, c * T + i * 128: c * T + (i + 1) * 128],
                    identity=identf[:]))
                ttok = p.ms("t")
                eng = "v" if c % 2 == 0 else "s"
                p.wait(eng, ttok)
                if eng == "v":
                    p.op("v", lambda v, c=c, bank=bank, b=buf: v.tensor_copy(
                        out=b[:, c * 128:(c + 1) * 128], in_=bank[:, 0:128]))
                else:
                    p.op("s", lambda s, c=c, bank=bank, b=buf: s.copy(
                        out=b[:, c * 128:(c + 1) * 128], in_=bank[:, 0:128]))
                p.wait("t", p.ms(eng))
            p.wait("q", p.ms("v"))
            p.wait("q", p.ms("s"))
            st_toks[i] = p.dma("q", y_d[i * 128:(i + 1) * 128, :], buf[:])

        p.dma_quiesce("q")
        p.dma_quiesce("g")
        p.finalize(block)
    return nc, sorted(tap_outs)


_NC_CACHE = {}


NLAYERS = 4


def kernel(**inputs):
    key = "v1"
    if key not in _NC_CACHE:
        _NC_CACHE[key] = build_program(nlayers=NLAYERS)
    nc, _ = _NC_CACHE[key]
    return _run(nc, inputs)


def _wnames(nl):
    names = ["L0_attn_gamma", "L0_attn_beta", "L0_attn_w_qkv",
             "L0_ffn_router_w", "L0_ffn_router_b", "L0_ffn_w1", "L0_ffn_b1",
             "L0_ffn_w2", "L0_ffn_b2"]
    if nl >= 2:
        names += ["L1_attn_Wq", "L1_attn_bq", "L1_attn_Wk", "L1_attn_bk",
                  "L1_attn_Wv", "L1_attn_bv", "L1_attn_Wproj", "L1_attn_bproj",
                  "L1_attn_wq_feat", "L1_attn_wk_feat",
                  "L1_ffn_router_w", "L1_ffn_router_b", "L1_ffn_w1",
                  "L1_ffn_w2", "L1_ffn_lin2_w", "L1_ffn_lin2_b"]
    if nl >= 3:
        names += ["L2_attn_proj_qk_w", "L2_attn_proj_qk_b", "L2_attn_proj_v_w",
                  "L2_attn_proj_v_b", "L2_attn_in_proj_w", "L2_attn_in_proj_b",
                  "L2_attn_out_proj_w", "L2_attn_out_proj_b",
                  "L2_ffn_W_r", "L2_ffn_W_k", "L2_ffn_W_v", "L2_ffn_mu_r", "L2_ffn_mu_k"]
    if nl >= 4:
        names += ["L3_attn_Wq", "L3_attn_Wk", "L3_attn_Wv", "L3_attn_Wg",
                  "L3_attn_Wo", "L3_attn_gn_weight", "L3_attn_gn_bias",
                  "L3_ffn_wqkv", "L3_ffn_q_scale", "L3_ffn_k_scale", "L3_ffn_wo",
                  "L3_ffn_ln_g", "L3_ffn_ln_b", "L3_ffn_w1", "L3_ffn_b1",
                  "L3_ffn_w2", "L3_ffn_b2"]
    return names


def _run(nc, inputs, extra_keys=(), nl=None):
    if nl is None:
        nl = NLAYERS
    xf = np.ascontiguousarray(inputs["L0_attn_x"], dtype=np.float32)
    wnames = _wnames(nl)
    in_maps = []
    for c in range(NCORES):
        m = {"xown": xf[c], "xfull": xf,
             "ownmask": np.eye(NCORES, dtype=np.float32)[c][None, :]}
        for n in wnames:
            m[n] = np.ascontiguousarray(inputs[n], dtype=np.float32)
        m["L0_ffn_router_b"] = m["L0_ffn_router_b"].reshape(1, 8)
        if nl >= 2:
            m["L1_ffn_router_b"] = m["L1_ffn_router_b"].reshape(1, 8)
            m["L1_attn_bv"] = m["L1_attn_bv"].reshape(1, D)
        in_maps.append(m)
    res = run_bass_kernel_spmd(nc, in_maps, core_ids=list(range(NCORES)))
    out = np.stack([np.asarray(res.results[c]["y"]) for c in range(NCORES)], axis=0)
    if extra_keys:
        extras = [{k: np.asarray(res.results[c]["tap_" + k]) for k in extra_keys}
                  for c in range(NCORES)]
        return out, extras
    return out
